# version 12
# baseline (speedup 1.0000x reference)
import numpy as np
import concourse.bass as bass
import concourse.mybir as mybir
from concourse.bass_utils import run_bass_kernel_spmd
from contextlib import ExitStack

F32 = mybir.dt.float32
BF16 = mybir.dt.bfloat16
I32 = mybir.dt.int32
AF = mybir.ActivationFunctionType
ALU = mybir.AluOpType
AX = mybir.AxisListType
IOA = bass.IndirectOffsetOnAxis

B = 2
N = 8192
D = 1024
HD = 64
TOK = 2048
NTOK = B * N
NE = 16
CAP = 1024
HID = 4096
EPC = 2
NQT = N // 512
NKB = N // 128
SCALE = 0.125
CSHIFT = 16.0
EPS = 1e-5
BIG = 1.0e9
NTT = TOK // 128
INV_D = 1.0 / D


class Ctr:
    def __init__(self, sem):
        self.sem = sem
        self.n = 0

    def inc(self, inst, k=1):
        inst.then_inc(self.sem, k)
        self.n += k
        return self.n

    def dma(self, inst):
        inst.then_inc(self.sem, 16)
        self.n += 16
        return self.n


_LDW_PATCHED = False


def _enable_ldw_opt():
    global _LDW_PATCHED
    if _LDW_PATCHED:
        return
    import concourse.bass_utils as _bu
    _real = _bu.run_command

    def _patched(cmd, **kw):
        cmd = [c.replace("--enable-ldw-opt=false", "--enable-ldw-opt=true")
               if isinstance(c, str) else c for c in cmd]
        return _real(cmd, **kw)

    _bu.run_command = _patched
    _LDW_PATCHED = True


def build_program(upto=99, dbg=False, _NODUMP=False):
    _enable_ldw_opt()
    nc = bass.Bass("TRN2", target_bir_lowering=False, debug=False)

    xt_d = nc.dram_tensor("xt", [D, N], F32, kind="ExternalInput").ap()
    xres_d = nc.dram_tensor("xres", [TOK, D], F32, kind="ExternalInput").ap()
    wqkv_d = nc.dram_tensor("wqkv_sel", [D, 768], F32, kind="ExternalInput").ap()
    wproj_d = nc.dram_tensor("wproj", [D, D], F32, kind="ExternalInput").ap()
    bproj_d = nc.dram_tensor("bprojr", [8, 128], F32, kind="ExternalInput").ap()
    wg_d = nc.dram_tensor("wg", [D, NE], F32, kind="ExternalInput").ap()
    bg_d = nc.dram_tensor("bg1", [1, NE], F32, kind="ExternalInput").ap()
    w1_d = nc.dram_tensor("w1", [EPC, D, HID], F32, kind="ExternalInput").ap()
    b1_d = nc.dram_tensor("b1r", [EPC, 32, 128], F32, kind="ExternalInput").ap()
    w2_d = nc.dram_tensor("w2", [EPC, HID, D], F32, kind="ExternalInput").ap()
    b2_d = nc.dram_tensor("b2r", [EPC, 8, 128], F32, kind="ExternalInput").ap()
    rankmask_d = nc.dram_tensor("rankmask", [8, 1], F32, kind="ExternalInput").ap()
    toff_d = nc.dram_tensor("toff", [128, 32], I32, kind="ExternalInput").ap()
    out_d = nc.dram_tensor("out", [TOK, D], F32, kind="ExternalOutput").ap()

    if dbg:
        dbg1 = nc.dram_tensor("dbg1", [128, 2048], F32, kind="ExternalOutput").ap()
        dbg2 = nc.dram_tensor("dbg2", [128, 2048], F32, kind="ExternalOutput").ap()
    cc_att_in = nc.dram_tensor("cc_att_in", [256, N], BF16)
    cc_att_out = nc.dram_tensor("cc_att_out", [D, N], BF16)
    x1spill = nc.dram_tensor("x1spill", [TOK, D], F32,
                              kind="ExternalOutput" if (dbg and upto == 3) else "Internal")
    xsort_loc = nc.dram_tensor("xsort_loc", [NTOK, D], BF16)
    xsort = nc.dram_tensor("xsort", [TOK, D], BF16)
    if dbg and upto == 4:
        dbgxs = nc.dram_tensor("dbgxs", [TOK, D], BF16, kind="ExternalOutput").ap()
        dbgsl = nc.dram_tensor("dbgsl", [128, NTT], I32, kind="ExternalOutput").ap()
    cc_cnt_in = nc.dram_tensor("cc_cnt_in", [1, NE], F32)
    cc_cnt_out = nc.dram_tensor("cc_cnt_out", [8, NE], F32, addr_space="Shared")
    cc_y_in = nc.dram_tensor("cc_y_in", [TOK, D], F32)
    if dbg and upto == 5:
        dbgy = nc.dram_tensor("dbgy", [TOK, D], F32, kind="ExternalOutput").ap()
        dbgxs5 = nc.dram_tensor("dbgxs5", [TOK, D], BF16, kind="ExternalOutput").ap()
        dbgh = nc.dram_tensor("dbgh", [128, 4096], BF16, kind="ExternalOutput").ap()
        dbgbt = nc.dram_tensor("dbgbt", [128, 2048], BF16, kind="ExternalOutput").ap()
    cc_y_out = nc.dram_tensor("cc_y_out", [NTOK, D], F32, addr_space="Shared")

    top = ExitStack()
    with top:
        T_ = top.enter_context

        ones128 = T_(nc.sbuf_tensor("ones128", [128, 128], F32))
        identb = T_(nc.sbuf_tensor("identb", [128, 128], BF16))
        identf = T_(nc.sbuf_tensor("identf", [128, 128], F32))
        triu = T_(nc.sbuf_tensor("triu", [128, 128], F32))
        iota16 = T_(nc.sbuf_tensor("iota16", [128, NE], F32))
        onescb = T_(nc.sbuf_tensor("onescb", [128, 1], BF16))
        onesrb = T_(nc.sbuf_tensor("onesrb", [1, 128], BF16))
        c_one = T_(nc.sbuf_tensor("c_one", [128, 1], F32))
        c_zero = T_(nc.sbuf_tensor("c_zero", [128, 1], F32))
        c_exps = T_(nc.sbuf_tensor("c_exps", [128, 1], F32))
        c_expb = T_(nc.sbuf_tensor("c_expb", [128, 1], F32))
        c_half = T_(nc.sbuf_tensor("c_half", [128, 1], F32))
        c_eps = T_(nc.sbuf_tensor("c_eps", [128, 1], F32))
        c_big = T_(nc.sbuf_tensor("c_big", [128, 1], F32))
        zerot = T_(nc.sbuf_tensor("zerot", [128, 1024], BF16))
        scr0 = T_(nc.sbuf_tensor("scr0", [128, 1024], F32))
        scr1 = T_(nc.sbuf_tensor("scr1", [128, 1024], F32))
        col0 = T_(nc.sbuf_tensor("col0", [128, 1], F32))
        col1 = T_(nc.sbuf_tensor("col1", [128, 1], F32))
        col2 = T_(nc.sbuf_tensor("col2", [128, 1], F32))
        col3 = T_(nc.sbuf_tensor("col3", [128, 1], F32))
        Lg = T_(nc.sbuf_tensor("Lg", [128, NE], F32))
        expl = T_(nc.sbuf_tensor("expl", [128, NE], F32))
        ohw = T_(nc.sbuf_tensor("ohw", [128, NE], F32))
        mrow = T_(nc.sbuf_tensor("mrow", [1, NE], F32))
        ohs = T_(nc.sbuf_tensor("ohs", [128, NTT * NE], F32))
        idxs = T_(nc.sbuf_tensor("idxs", [128, NTT], F32))
        gates = T_(nc.sbuf_tensor("gates", [128, NTT], F32))
        slots = T_(nc.sbuf_tensor("slots", [128, NTT], I32))
        posf = T_(nc.sbuf_tensor("posf", [128, 1], F32))
        keepm = T_(nc.sbuf_tensor("keepm", [128, 1], F32))
        slotf = T_(nc.sbuf_tensor("slotf", [128, 1], F32))
        slotg = T_(nc.sbuf_tensor("slotg", [128, 1], F32))
        basetile = T_(nc.sbuf_tensor("basetile", [1, NTT * NE], F32))
        basecore = T_(nc.sbuf_tensor("basecore", [1, NE], F32))
        cnts_all = T_(nc.sbuf_tensor("cnts_all", [8, NE], F32))
        cntrow = T_(nc.sbuf_tensor("cntrow", [1, NE], F32))
        rankm = T_(nc.sbuf_tensor("rankm", [8, 1], F32))
        bgb = T_(nc.sbuf_tensor("bgb", [1, NE], F32))
        wgb = T_(nc.sbuf_tensor("wgb", [128, 8 * NE], F32))
        bprojT = T_(nc.sbuf_tensor("bprojT", [128, 8], F32))
        bprojS = T_(nc.sbuf_tensor("bprojS", [8, 128], F32))

        lds = Ctr(T_(nc.semaphore("lds")))
        ldg = Ctr(T_(nc.semaphore("ldg")))
        st = Ctr(T_(nc.semaphore("st")))
        ss = Ctr(T_(nc.semaphore("ss")))
        sv = Ctr(T_(nc.semaphore("sv")))
        sg = Ctr(T_(nc.semaphore("sg")))
        cc = Ctr(T_(nc.semaphore("cc")))
        sqk = Ctr(T_(nc.semaphore("sqk")))
        sxp = Ctr(T_(nc.semaphore("sxp")))
        so = Ctr(T_(nc.semaphore("so")))

        block = T_(nc.Block())

        @block.vector
        def _(v):
            v.memset(ones128[:, :], 1.0)
            v.memset(onescb[:, :], 1.0)
            v.memset(onesrb[:, :], 1.0)
            v.memset(c_one[:, :], 1.0)
            v.memset(c_zero[:, :], 0.0)
            v.memset(c_exps[:, :], SCALE)
            v.memset(c_expb[:, :], -CSHIFT)
            v.memset(c_half[:, :], -0.5)
            v.memset(c_eps[:, :], EPS)
            v.memset(c_big[:, :], BIG)
            v.memset(zerot[:, :], 0.0)
            vi = v.memset(basetile[:, 0:NE], 0.0)
            sv.inc(vi)
        SV_CONST = sv.n

        @block.gpsimd
        def _(g):
            g.wait_ge(sv.sem, SV_CONST)
            g.iota(iota16[:, :], pattern=[[1, NE]], base=0, channel_multiplier=0,
                   allow_small_or_imprecise_dtypes=True)
            g.affine_select(identb[:, :], ones128[:, :], pattern=[[-1, 128]],
                            compare_op=ALU.is_equal, fill=0.0, base=0, channel_multiplier=1)
            g.affine_select(identf[:, :], ones128[:, :], pattern=[[-1, 128]],
                            compare_op=ALU.is_equal, fill=0.0, base=0, channel_multiplier=1)
            gi = g.affine_select(triu[:, :], ones128[:, :], pattern=[[1, 128]],
                                 compare_op=ALU.is_gt, fill=0.0, base=0, channel_multiplier=-1)
            sg.inc(gi)
        SG_CONST = sg.n

        @block.sync
        def _(s):
            lds.dma(s.dma_start(out=wgb[:, :].rearrange("p (cc e) -> p cc e", cc=8),
                                in_=wg_d.rearrange("(cc p) e -> p cc e", p=128)))
            lds.dma(s.dma_start(out=bgb[:, :], in_=bg_d))

        @block.sync
        def _(s):
            lds.dma(s.dma_start(out=bprojS[:, :], in_=bproj_d))
            lds.dma(s.dma_start(out=rankm[:, :], in_=rankmask_d))
        LDS_W0 = lds.n

        @block.sync
        def _(s):
            s.wait_ge(sv.sem, SV_CONST)
            for i in range(NTOK // 128):
                lds.dma(s.dma_start(out=xsort_loc[128 * i:128 * (i + 1), :], in_=zerot[:, :]))
        LDS_ZERO = lds.n

        scopeA = top.enter_context(ExitStack())
        A_ = scopeA.enter_context
        kTp = [A_(nc.sbuf_tensor(f"kTp{p}", [128, N], BF16)) for p in range(2)]
        qTp = [A_(nc.sbuf_tensor(f"qTp{p}", [128, N], BF16)) for p in range(2)]
        vh = A_(nc.sbuf_tensor("vh", [128, NKB * 4 * 65], BF16))
        rowt = A_(nc.sbuf_tensor("rowt", [128, 1024], F32))
        obuf2 = A_(nc.sbuf_tensor("obuf2", [64, 512], BF16))
        ocopyA = A_(nc.sbuf_tensor("ocopyA", [128, 512], F32))
        ocopyB = A_(nc.sbuf_tensor("ocopyB", [128, 512], F32))
        erow = A_(nc.sbuf_tensor("erow", [128, 512], F32))
        erow2 = A_(nc.sbuf_tensor("erow2", [128, 512], F32))

        scope1 = top.enter_context(ExitStack())
        S1_ = scope1.enter_context
        wqkvb = S1_(nc.sbuf_tensor("wqkvb", [128, 8 * 768], BF16))
        xtq = [S1_(nc.sbuf_tensor(f"xtq{i}", [128, 8 * 512], BF16)) for i in range(2)]
        psQ = S1_(nc.psum_tensor("psQ", [128, 2048], F32))
        psV = S1_(nc.psum_tensor("psV", [128, 1024], F32))

        @block.gpsimd
        def _(g):
            ldg.dma(g.dma_start(out=wqkvb[:, :].rearrange("p (cc o) -> p cc o", cc=8),
                                in_=wqkv_d.rearrange("(cc p) o -> p cc o", p=128)))
        LDG_WQKV = ldg.n

        @block.vector
        def _(v):
            vi = v.memset(vh[:, :], 1.0)
            sv.inc(vi)

        ldg_x = []
        qk_done = []
        v_done = []
        ev_qk = []
        ev_v = []
        for tt in range(16):
            @block.gpsimd
            def _(g):
                if tt >= 2:
                    g.wait_ge(st.sem, v_done[tt - 2])
                ldg.dma(g.dma_start(
                    out=xtq[tt % 2][:, :].rearrange("p (cc t) -> p cc t", cc=8),
                    in_=xt_d[:, 512 * tt:512 * (tt + 1)].rearrange("(cc p) t -> p cc t", p=128)))
            ldg_x.append(ldg.n)

            @block.tensor
            def _(t):
                t.wait_ge(ldg.sem, ldg_x[tt])
                if tt == 0:
                    t.wait_ge(ldg.sem, LDG_WQKV)
                    t.wait_ge(sg.sem, SG_CONST)
                    t.wait_ge(sv.sem, SV_CONST)
                if tt >= 1:
                    t.wait_ge(ss.sem, ev_qk[tt - 1])
                for ccc in range(8):
                    xs = xtq[tt % 2][:, ccc * 512:(ccc + 1) * 512]
                    fl = dict(start=(ccc == 0), stop=(ccc == 7))
                    t.matmul(psQ[:, 0:512], wqkvb[:, ccc * 768 + 0:ccc * 768 + 128], xs, **fl)
                    t.matmul(psQ[:, 512:1024], wqkvb[:, ccc * 768 + 128:ccc * 768 + 256], xs, **fl)
                    t.matmul(psQ[:, 1024:1536], wqkvb[:, ccc * 768 + 256:ccc * 768 + 384], xs, **fl)
                    ti = t.matmul(psQ[:, 1536:2048], wqkvb[:, ccc * 768 + 384:ccc * 768 + 512], xs, **fl)
                st.inc(ti)
                qk_done.append(st.n)
                if tt >= 1:
                    t.wait_ge(sv.sem, ev_v[tt - 1])
                for vsub in range(4):
                    vps = psV[:, vsub * 256:(vsub + 1) * 256]
                    for ccc in range(8):
                        ti = t.matmul(vps,
                                      xtq[tt % 2][:, ccc * 512 + vsub * 128:ccc * 512 + vsub * 128 + 128],
                                      wqkvb[:, ccc * 768 + 512:ccc * 768 + 768],
                                      start=(ccc == 0), stop=(ccc == 7))
                st.inc(ti)
                v_done.append(st.n)

            @block.scalar
            def _(sc):
                sc.wait_ge(st.sem, qk_done[tt])
                qc = slice(tt * 512, tt * 512 + 512)
                sc.activation(qTp[0][:, qc], psQ[:, 0:512], AF.Identity,
                              bias=c_zero[:, 0:1], scale=c_one[:, 0:1])
                sc.activation(qTp[1][:, qc], psQ[:, 512:1024], AF.Identity,
                              bias=c_zero[:, 0:1], scale=c_one[:, 0:1])
                sc.activation(kTp[0][:, qc], psQ[:, 1024:1536], AF.Identity,
                              bias=c_zero[:, 0:1], scale=c_one[:, 0:1])
                si = sc.activation(kTp[1][:, qc], psQ[:, 1536:2048], AF.Identity,
                                   bias=c_zero[:, 0:1], scale=c_one[:, 0:1])
                ss.inc(si)
                ev_qk.append(ss.n)

            @block.vector
            def _(v):
                v.wait_ge(st.sem, v_done[tt])
                for vsub in range(4):
                    kb = tt * 4 + vsub
                    vps = psV[:, vsub * 256:(vsub + 1) * 256]
                    for h in range(4):
                        vi = v.tensor_copy(vh[:, (kb * 4 + h) * 65:(kb * 4 + h) * 65 + HD],
                                           vps[:, h * HD:(h + 1) * HD])
                sv.inc(vi)
                ev_v.append(sv.n)

        if upto <= 1:
            if dbg:
                @block.gpsimd
                def _(g):
                    g.wait_ge(ss.sem, ev_qk[-1])
                    g.wait_ge(sv.sem, ev_v[-1])
                    ldg.dma(g.dma_start(out=dbg1[:, 0:1024], in_=qTp[0][:, 0:1024]))
                    ldg.dma(g.dma_start(out=dbg1[:, 1024:2048], in_=kTp[1][:, 0:1024]))
                    ldg.dma(g.dma_start(out=dbg2[:, 0:1024], in_=vh[:, 0:1024]))
                    ldg.dma(g.dma_start(out=dbg2[:, 1024:2048], in_=qTp[1][:, 0:1024]))
                    g.wait_ge(ldg.sem, ldg.n)
            else:
                @block.sync
                def _(s):
                    s.wait_ge(ss.sem, ev_qk[-1])
                    s.wait_ge(sv.sem, ev_v[-1])
            return nc
        scope1.close()

        oall = [A_(nc.sbuf_tensor(f"oall{p}", [128, N], BF16)) for p in range(2)]
        pbuf = [A_(nc.sbuf_tensor(f"pbuf{i}", [128, 1024], BF16)) for i in range(2)]
        scope2 = top.enter_context(ExitStack())
        S2_ = scope2.enter_context
        psS = [S2_(nc.psum_tensor(f"psS{i}", [128, 1024], F32)) for i in range(2)]
        psOA = S2_(nc.psum_tensor("psOA", [128, 512], F32))
        psOB = S2_(nc.psum_tensor("psOB", [128, 512], F32))
        psEpA = S2_(nc.psum_tensor("psEpA", [128, 512], F32))
        psEpB = S2_(nc.psum_tensor("psEpB", [128, 512], F32))

        PH1_SS = ss.n
        PH1_SV = sv.n
        NIT = 2 * NQT * NKB

        pv_marks = {}
        ob_marks = []
        omul_marks = {}
        ep_prev_ocopy = {}
        rc_marks = {}
        bc_marks = {}
        ecp_marks = {}
        ep_marks = {}

        def emit_qk(t, j, skip_wait=False):
            p, r = j // (NQT * NKB), j % (NQT * NKB)
            qt, kb = r // NKB, r % NKB
            if j >= 2 and not skip_wait:
                t.wait_ge(sxp.sem, j - 1)
            if j == 0:
                t.wait_ge(ss.sem, PH1_SS)
                t.wait_ge(sv.sem, PH1_SV)
            kc = slice(kb * 128, kb * 128 + 128)
            qc = slice(qt * 512, qt * 512 + 512)
            t.matmul(psS[j % 2][:, 0:512], kTp[p][0:64, kc], qTp[p][0:64, qc],
                     start=True, stop=True, tile_position=(0, 0))
            ti = t.matmul(psS[j % 2][:, 512:1024], kTp[p][64:128, kc], qTp[p][64:128, qc],
                          start=True, stop=True, tile_position=(64, 0))
            sqk.inc(ti)

        def emit_pv(t, j):
            p, r = j // (NQT * NKB), j % (NQT * NKB)
            qt, kb = r // NKB, r % NKB
            t.wait_ge(sxp.sem, j + 1)
            if kb == 0 and (p, qt) != (0, 0):
                prev = (p, qt - 1) if qt > 0 else (p - 1, NQT - 1)
                t.wait_ge(sv.sem, ep_marks[prev])
            pb = pbuf[j % 2]
            first = (kb == 0)
            last = (kb == NKB - 1)
            t.matmul(psOA[0:65, :], vh[:, (kb * 4 + 2 * p) * 65:(kb * 4 + 2 * p) * 65 + 65],
                     pb[:, 0:512], start=first, stop=last)
            ti = t.matmul(psOB[0:65, :], vh[:, (kb * 4 + 2 * p + 1) * 65:(kb * 4 + 2 * p + 1) * 65 + 65],
                          pb[:, 512:1024], start=first, stop=last)
            if last:
                st.inc(ti)
                pv_marks[(p, qt)] = st.n

        def emit_ep_recip(pe, qe):
            @block.vector
            def _(v):
                v.wait_ge(st.sem, pv_marks[(pe, qe)])
                if (pe, qe) in ep_prev_ocopy:
                    v.wait_ge(sv.sem, ep_prev_ocopy[(pe, qe)])
                v.tensor_copy(ocopyA[0:65, :], psOA[0:65, :])
                vi = v.tensor_copy(ocopyB[0:65, :], psOB[0:65, :])
                sv.inc(vi)
                ep_marks[(pe, qe)] = sv.n
                v.wait_ge(sv.sem, sv.n)
                v.reciprocal(rowt[64:65, 0:512], ocopyA[64:65, :])
                vi = v.reciprocal(rowt[64:65, 512:1024], ocopyB[64:65, :])
                sv.inc(vi)
                rc_marks[(pe, qe)] = sv.n

        def emit_ep_bcast(t, pe, qe):
            t.wait_ge(sv.sem, rc_marks[(pe, qe)])
            if (pe, qe) != (0, 0):
                prev = (pe, qe - 1) if qe > 0 else (pe - 1, NQT - 1)
                t.wait_ge(sv.sem, ecp_marks[prev])
            t.matmul(psEpA[0:64, :], ones128[64:65, 0:64], rowt[64:65, 0:512],
                     start=True, stop=True, tile_position=(64, 0))
            ti = t.matmul(psEpB[0:64, :], ones128[64:65, 0:64], rowt[64:65, 512:1024],
                          start=True, stop=True, tile_position=(64, 0))
            st.inc(ti)
            bc_marks[(pe, qe)] = st.n

        def emit_ep_mult(pe, qe):
            @block.vector
            def _(v):
                v.wait_ge(st.sem, bc_marks[(pe, qe)])
                if ob_marks:
                    v.wait_ge(lds.sem, ob_marks[-1])
                v.tensor_copy(erow[0:64, :], psEpA[0:64, :])
                vi = v.tensor_copy(erow2[0:64, :], psEpB[0:64, :])
                sv.inc(vi)
                ecp_marks[(pe, qe)] = sv.n
                v.wait_ge(sv.sem, ecp_marks[(pe, qe)])
                qc = slice(qe * 512, qe * 512 + 512)
                v.tensor_tensor(oall[pe][0:64, qc], ocopyA[0:64, :], erow[0:64, :], ALU.mult)
                vi = v.tensor_tensor(obuf2[0:64, :], ocopyB[0:64, :], erow2[0:64, :], ALU.mult)
                sv.inc(vi)
                omul_marks[(pe, qe)] = sv.n
                nxt = (pe, qe + 1) if qe + 1 < NQT else (pe + 1, 0)
                ep_prev_ocopy[nxt] = sv.n

            @block.sync
            def _(s):
                s.wait_ge(sv.sem, omul_marks[(pe, qe)])
                qc = slice(qe * 512, qe * 512 + 512)
                lds.dma(s.dma_start(out=oall[pe][64:128, qc], in_=obuf2[0:64, :]))
                ob_marks.append(lds.n)

        for p in range(2):
            for qt in range(NQT):
                j0 = (p * NQT + qt) * NKB
                prev = None if (p, qt) == (0, 0) else ((p, qt - 1) if qt > 0 else (p - 1, NQT - 1))

                @block.tensor
                def _(t):
                    emit_qk(t, j0)
                    if j0 >= 1:
                        emit_pv(t, j0 - 1)
                if prev is not None:
                    emit_ep_recip(*prev)

                    @block.tensor
                    def _(t):
                        emit_ep_bcast(t, *prev)
                    emit_ep_mult(*prev)

                @block.tensor
                def _(t):
                    for kb in range(1, NKB):
                        emit_qk(t, j0 + kb)
                        emit_pv(t, j0 + kb - 1)

                @block.scalar
                def _(sc):
                    for kb in range(NKB):
                        j = j0 + kb
                        sc.wait_ge(sqk.sem, j + 1)
                        si = sc.activation(pbuf[j % 2][:, :], psS[j % 2][:, 0:1024], AF.Exp,
                                           bias=c_expb[:, 0:1], scale=c_exps[:, 0:1])
                        sxp.inc(si)

        @block.tensor
        def _(t):
            emit_pv(t, NIT - 1)
        emit_ep_recip(1, NQT - 1)

        @block.tensor
        def _(t):
            emit_ep_bcast(t, 1, NQT - 1)
        emit_ep_mult(1, NQT - 1)
        EP_FINAL = omul_marks[(1, NQT - 1)]

        @block.sync
        def _(s):
            s.wait_ge(sv.sem, EP_FINAL)
            lds.dma(s.dma_start(out=cc_att_in[0:128, :], in_=oall[0][:, :]))
            lds.dma(s.dma_start(out=cc_att_in[128:256, :], in_=oall[1][:, :]))
        LDS_OALL = lds.n

        @block.gpsimd
        def _(g):
            g.wait_ge(lds.sem, LDS_OALL)
            gi = g.collective_compute(
                "AllGather", ALU.bypass, replica_groups=[[0, 1, 2, 3], [4, 5, 6, 7]],
                ins=[cc_att_in[:, :]], outs=[cc_att_out[:, :]])
            cc.inc(gi)
        CC_ATT = cc.n

        if upto <= 2:
            if dbg:
                @block.gpsimd
                def _(g):
                    g.wait_ge(cc.sem, CC_ATT)
                    ldg.dma(g.dma_start(out=dbg1[:, 0:2048], in_=oall[0][:, 0:2048]))
                    ldg.dma(g.dma_start(out=dbg2[:, 0:2048], in_=oall[1][:, 0:2048]))
                    g.wait_ge(ldg.sem, ldg.n)
            else:
                @block.sync
                def _(s):
                    s.wait_ge(cc.sem, CC_ATT)
            return nc

        PH2_ST = st.n
        PH2_SS = ss.n
        scope2.close()
        scopeA.close()

        scope3 = top.enter_context(ExitStack())
        S3_ = scope3.enter_context
        wprojb = S3_(nc.sbuf_tensor("wprojb", [128, 8 * 1024], BF16))
        ccTt = S3_(nc.sbuf_tensor("ccTt", [128, 8 * 512], BF16))
        aprojT = S3_(nc.sbuf_tensor("aprojT", [128, 8 * 512], F32))
        xres_t = [S3_(nc.sbuf_tensor(f"xres{i}", [128, 1024], F32)) for i in range(2)]
        x1b = S3_(nc.sbuf_tensor("x1b", [128, NTT * 1024], BF16))
        x1Tt = S3_(nc.sbuf_tensor("x1Tt", [128, 1024], F32))
        xf_t = [S3_(nc.sbuf_tensor(f"xf_t{i}", [128, 1024], F32)) for i in range(2)]
        idxg = S3_(nc.sbuf_tensor("idxg", [128, 32], I32))
        toffb = S3_(nc.sbuf_tensor("toffb", [128, 32], I32))
        scope3m = top.enter_context(ExitStack())
        S3m_ = scope3m.enter_context
        psPr = S3m_(nc.psum_tensor("psPr", [128, 2048], F32))
        psTr = S3m_(nc.psum_tensor("psTr", [128, 2048], F32))

        @block.gpsimd
        def _(g):
            g.wait_ge(lds.sem, LDS_OALL)
            ldg.dma(g.dma_start(out=toffb[:, :], in_=toff_d))
            for ccc in range(8):
                for tt4 in range(4):
                    gi = g.iota(idxg[:, ccc * 4 + tt4:ccc * 4 + tt4 + 1], pattern=[[0, 1]],
                                base=ccc * 128 * N + tt4 * 512, channel_multiplier=N)
            sg.inc(gi)
            ldg.dma(g.dma_start(out=wprojb[:, :].rearrange("p (cc o) -> p cc o", cc=8),
                                in_=wproj_d.rearrange("(cc p) o -> p cc o", p=128)))
        SG_IDX = sg.n
        LDG_WPROJ = ldg.n

        @block.vector
        def _(v):
            v.wait_ge(sg.sem, SG_IDX)
            v.wait_ge(ldg.sem, LDG_WPROJ)
            vi = v.tensor_tensor(idxg[:, :], idxg[:, :], toffb[:, :], ALU.add)
            sv.inc(vi)
        SV_IDX = sv.n

        @block.tensor
        def _(t):
            t.wait_ge(lds.sem, LDS_W0)
            t.wait_ge(ss.sem, PH2_SS)
            t.wait_ge(sv.sem, EP_FINAL)
            t.wait_ge(sg.sem, SG_CONST)
            ti = t.transpose(psTr[:, 0:8], bprojS[0:8, :], identf[0:8, 0:8])
            st.inc(ti)
        ST_BPJ = st.n

        @block.vector
        def _(v):
            v.wait_ge(st.sem, ST_BPJ)
            vi = v.tensor_copy(bprojT[:, 0:8], psTr[:, 0:8])
            sv.inc(vi)
        SV_BPJ = sv.n

        x1_done = []
        ldg_cct = []
        pr_evac = []
        prmm_last = []
        for tt4 in range(4):
            @block.gpsimd
            def _(g):
                g.wait_ge(cc.sem, CC_ATT)
                g.wait_ge(sv.sem, SV_IDX)
                if tt4 >= 1:
                    g.wait_ge(st.sem, prmm_last[tt4 - 1])
                for ccc in range(8):
                    ldg.dma(g.indirect_dma_start(
                        out=ccTt[:, ccc * 512:(ccc + 1) * 512], out_offset=None,
                        in_=cc_att_out[:, :],
                        in_offset=IOA(idxg[:, ccc * 4 + tt4:ccc * 4 + tt4 + 1], 1)))
            ldg_cct.append(ldg.n)

            ss_base = ss.n
            prmm_marks = []
            for cs in range(8):
                @block.tensor
                def _(t):
                    t.wait_ge(ldg.sem, ldg_cct[tt4])
                    if tt4 == 0 and cs == 0:
                        t.wait_ge(sv.sem, SV_BPJ)
                    gi = tt4 * 8 + cs
                    if gi >= 4:
                        t.wait_ge(ss.sem, ss_base + cs - 3 if cs >= 4 else pr_evac[gi - 4])
                    bank = psPr[:, (cs % 4) * 512:(cs % 4) * 512 + 512]
                    for ccc in range(8):
                        ti = t.matmul(bank,
                                      wprojb[:, ccc * 1024 + cs * 128:ccc * 1024 + cs * 128 + 128],
                                      ccTt[:, ccc * 512:(ccc + 1) * 512],
                                      start=(ccc == 0), stop=(ccc == 7))
                    st.inc(ti)
                prmm_marks.append(st.n)
                if cs == 7:
                    prmm_last.append(st.n)

            @block.scalar
            def _(sc):
                for cs in range(8):
                    sc.wait_ge(st.sem, prmm_marks[cs])
                    bank = psPr[:, (cs % 4) * 512:(cs % 4) * 512 + 512]
                    si = sc.activation(aprojT[:, cs * 512:(cs + 1) * 512], bank, AF.Identity,
                                       bias=bprojT[:, cs:cs + 1], scale=c_one[:, 0:1])
                    ss.inc(si)
                    pr_evac.append(ss.n)

            for tsub in range(4):
                ti128 = tt4 * 4 + tsub

                @block.sync
                def _(s):
                    if ti128 >= 2:
                        s.wait_ge(sv.sem, x1_done[ti128 - 2])
                    lds.dma(s.dma_start(out=xres_t[tsub % 2][:, :],
                                        in_=xres_d[ti128 * 128:(ti128 + 1) * 128, :]))
                LDS_XR = lds.n

                @block.tensor
                def _(t):
                    t.wait_ge(ss.sem, pr_evac[tt4 * 8 + 7])
                    if ti128 >= 2:
                        t.wait_ge(sv.sem, x1_done[ti128 - 2])
                    half = psTr[:, (tsub % 2) * 1024:(tsub % 2) * 1024 + 1024]
                    for cs in range(8):
                        ti = t.transpose(half[:, cs * 128:(cs + 1) * 128],
                                         aprojT[:, cs * 512 + tsub * 128:cs * 512 + tsub * 128 + 128],
                                         identf[:, :])
                    st.inc(ti)
                ST_TR = st.n

                @block.vector
                def _(v):
                    v.wait_ge(st.sem, ST_TR)
                    v.wait_ge(lds.sem, LDS_XR)
                    half = psTr[:, (tsub % 2) * 1024:(tsub % 2) * 1024 + 1024]
                    vi = v.tensor_tensor(scr0[:, :], half[:, 0:1024], xres_t[tsub % 2][:, :], ALU.add)
                    sv.inc(vi)
                    v.wait_ge(sv.sem, sv.n)
                    vi = v.reduce_sum(col0[:, :], scr0[:, :], axis=AX.X)
                    sv.inc(vi)
                    v.wait_ge(sv.sem, sv.n)
                    vi = v.tensor_scalar_mul(col1[:, :], col0[:, :], INV_D)
                    sv.inc(vi)
                    v.wait_ge(sv.sem, sv.n)
                    vi = v.tensor_scalar(scr1[:, :], scr0[:, :], col1[:, 0:1], None, ALU.subtract)
                    sv.inc(vi)
                SV_XC = sv.n

                @block.scalar
                def _(sc):
                    sc.wait_ge(sv.sem, SV_XC)
                    si = sc.activation(scr0[:, :], scr1[:, :], AF.Square,
                                       bias=c_zero[:, 0:1], scale=c_one[:, 0:1],
                                       accum_out=col2[:, :])
                    ss.inc(si)
                    sc.wait_ge(sv.sem, sv.n + 1)
                    si = sc.activation(col3[:, :], col2[:, :], AF.Ln,
                                       bias=c_eps[:, 0:1], scale=c_one[:, 0:1])
                    ss.inc(si)
                    sc.wait_ge(ss.sem, ss.n)
                    si = sc.activation(col3[:, :], col3[:, :], AF.Exp,
                                       bias=c_zero[:, 0:1], scale=c_half[:, 0:1])
                    ss.inc(si)
                SS_SQ = ss.n - 2

                @block.vector
                def _(v):
                    v.wait_ge(ss.sem, SS_SQ)
                    vi = v.tensor_scalar_mul(col2[:, :], col2[:, :], INV_D)
                    sv.inc(vi)
                SS_RSTD = SS_SQ + 2

                @block.vector
                def _(v):
                    v.wait_ge(ss.sem, SS_RSTD)
                    vi = v.tensor_scalar(scr0[:, :], scr1[:, :], col3[:, 0:1], None, ALU.mult)
                    sv.inc(vi)
                    v.wait_ge(sv.sem, sv.n)
                    vi = v.tensor_copy(x1b[:, ti128 * 1024:(ti128 + 1) * 1024], scr0[:, :])
                    sv.inc(vi)
                SV_X1 = sv.n

                @block.sync
                def _(s):
                    s.wait_ge(sv.sem, SV_X1)
                    lds.dma(s.dma_start(out=x1spill[ti128 * 128:(ti128 + 1) * 128, :],
                                        in_=scr0[:, :]))
                x1_done.append(SV_X1)

        if upto <= 3:
            @block.sync
            def _(s):
                s.wait_ge(lds.sem, lds.n)
                s.wait_ge(sv.sem, sv.n)
            return nc
        scope3m.close()
        scope4a = top.enter_context(ExitStack())
        S4a_ = scope4a.enter_context
        psLgT = S4a_(nc.psum_tensor("psLgT", [128, 1024], F32))
        psLg = S4a_(nc.psum_tensor("psLg", [128, 512], F32))
        psCnt = S4a_(nc.psum_tensor("psCnt", [1, 512], F32))
        psPos = S4a_(nc.psum_tensor("psPos", [128, 512], F32))

        g_marks = []
        lg_marks = []
        x1t_marks = []
        x1t_marks_v = []
        PH3_ST = st.n
        PH3_SS = ss.n
        for k in range(NTT):
            @block.sync
            def _(s):
                if k >= 2:
                    s.wait_ge(st.sem, x1t_marks[k - 2])
                lds.dma(s.dma_start(out=xf_t[k % 2][:, :],
                                    in_=x1spill[k * 128:(k + 1) * 128, :]))
            LDS_XF = lds.n

            @block.tensor
            def _(t):
                t.wait_ge(sv.sem, x1_done[k])
                t.wait_ge(lds.sem, LDS_XF)
                if k == 0:
                    t.wait_ge(ss.sem, PH3_SS)
                else:
                    t.wait_ge(sv.sem, x1t_marks_v[k - 1])
                for ccc in range(8):
                    ti = t.transpose(psLgT[:, ccc * 128:(ccc + 1) * 128],
                                     xf_t[k % 2][:, ccc * 128:(ccc + 1) * 128],
                                     identf[:, :])
                st.inc(ti)
                x1t_marks.append(st.n)
            ST_X1T = st.n

            @block.vector
            def _(v):
                v.wait_ge(st.sem, ST_X1T)
                vi = v.tensor_copy(x1Tt[:, :], psLgT[:, 0:1024])
                sv.inc(vi)
                x1t_marks_v.append(sv.n)
            SV_X1T = sv.n

            @block.tensor
            def _(t):
                t.wait_ge(sv.sem, SV_X1T)
                if k >= 1:
                    t.wait_ge(sv.sem, lg_marks[k - 1])
                for ccc in range(8):
                    t.matmul(psLg[:, 0:NE], x1Tt[:, ccc * 128:(ccc + 1) * 128],
                             wgb[:, ccc * NE:(ccc + 1) * NE],
                             start=(ccc == 0), stop=False)
                ti = t.matmul(psLg[:, 0:NE], ones128[0:1, :], bgb[:, :], start=False, stop=True)
                st.inc(ti)
            ST_LG = st.n

            @block.vector
            def _(v):
                v.wait_ge(st.sem, ST_LG)
                vi = v.tensor_copy(Lg[:, :], psLg[:, 0:NE])
                sv.inc(vi)
                lg_marks.append(sv.n)
                v.wait_ge(sv.sem, sv.n)
                vi = v.reduce_max(col0[:, :], Lg[:, :], axis=AX.X)
                sv.inc(vi)
                v.wait_ge(sv.sem, sv.n)
                vi = v.tensor_scalar_mul(col1[:, :], col0[:, :], -1.0)
                sv.inc(vi)
                vi = v.tensor_scalar(ohw[:, :], Lg[:, :], col0[:, 0:1], None, ALU.is_ge)
                sv.inc(vi)
            SV_OH = sv.n

            @block.scalar
            def _(sc):
                sc.wait_ge(sv.sem, SV_OH)
                si = sc.activation(expl[:, :], Lg[:, :], AF.Exp,
                                   bias=col1[:, 0:1], scale=c_one[:, 0:1],
                                   accum_out=col2[:, :])
                ss.inc(si)
            SS_EXPL = ss.n

            @block.vector
            def _(v):
                v.wait_ge(ss.sem, SS_EXPL)
                vi = v.reciprocal(gates[:, k:k + 1], col2[:, :])
                sv.inc(vi)
                vi = v.tensor_copy(ohs[:, k * NE:(k + 1) * NE], ohw[:, :])
                sv.inc(vi)
                v.wait_ge(sv.sem, sv.n)
                vi = v.tensor_tensor(expl[:, :], ohw[:, :], iota16[:, :], ALU.mult)
                sv.inc(vi)
                v.wait_ge(sv.sem, sv.n)
                vi = v.reduce_sum(idxs[:, k:k + 1], expl[:, :], axis=AX.X)
                sv.inc(vi)
                if k >= 1:
                    v.wait_ge(st.sem, g_marks[k - 1])
                    vi = v.tensor_copy(basetile[:, k * NE:(k + 1) * NE], psCnt[0:1, 0:NE])
                    sv.inc(vi)
            SV_BT = sv.n

            @block.tensor
            def _(t):
                t.wait_ge(sv.sem, SV_BT)
                ti = t.matmul(psCnt[0:1, 0:NE], c_one[:, 0:1], ohs[:, k * NE:(k + 1) * NE],
                              start=(k == 0), stop=(k == NTT - 1))
                st.inc(ti)
            g_marks.append(st.n)

        @block.vector
        def _(v):
            v.wait_ge(st.sem, g_marks[-1])
            vi = v.tensor_copy(cntrow[:, :], psCnt[0:1, 0:NE])
            sv.inc(vi)
        SV_CNT = sv.n

        @block.sync
        def _(s):
            s.wait_ge(sv.sem, SV_CNT)
            lds.dma(s.dma_start(out=cc_cnt_in.ap(), in_=cntrow[:, :]))
        LDS_CNT = lds.n

        @block.gpsimd
        def _(g):
            g.wait_ge(lds.sem, LDS_CNT)
            gi = g.collective_compute("AllGather", ALU.bypass, replica_groups=[list(range(8))],
                                      ins=[cc_cnt_in[:, :]], outs=[cc_cnt_out[:, :]])
            cc.inc(gi)
            g.wait_ge(cc.sem, cc.n)
            ldg.dma(g.dma_start(out=cnts_all[:, :], in_=cc_cnt_out.ap()))
        LDG_CNTS = ldg.n

        @block.tensor
        def _(t):
            t.wait_ge(ldg.sem, LDG_CNTS)
            t.wait_ge(sv.sem, SV_CNT)
            ti = t.matmul(psCnt[0:1, 16:32], rankm[:, :], cnts_all[:, :],
                          start=True, stop=True)
            st.inc(ti)
        ST_BC = st.n

        @block.vector
        def _(v):
            v.wait_ge(st.sem, ST_BC)
            vi = v.tensor_copy(basecore[:, :], psCnt[0:1, 16:32])
            sv.inc(vi)
        SV_BC = sv.n

        sc_marks = []
        for k in range(NTT):
            @block.vector
            def _(v):
                v.wait_ge(sv.sem, SV_BC)
                if k >= 1:
                    v.wait_ge(st.sem, sc_marks[k - 1][0])
                vi = v.tensor_tensor(mrow[:, :], basecore[:, :],
                                     basetile[:, k * NE:(k + 1) * NE], ALU.add)
                sv.inc(vi)
            SV_MR = sv.n

            @block.tensor
            def _(t):
                t.wait_ge(sv.sem, SV_MR)
                t.matmul(psPos[:, 0:NE], triu[:, :], ohs[:, k * NE:(k + 1) * NE],
                         start=True, stop=False)
                ti = t.matmul(psPos[:, 0:NE], ones128[0:1, :], mrow[:, :],
                              start=False, stop=True)
                st.inc(ti)
            ST_POS = st.n

            @block.vector
            def _(v):
                v.wait_ge(st.sem, ST_POS)
                vi = v.tensor_tensor(expl[:, :], psPos[:, 0:NE], ohs[:, k * NE:(k + 1) * NE],
                                     ALU.mult)
                sv.inc(vi)
                v.wait_ge(sv.sem, sv.n)
                vi = v.reduce_sum(posf[:, :], expl[:, :], axis=AX.X)
                sv.inc(vi)
                v.wait_ge(sv.sem, sv.n)
                vi = v.tensor_scalar(keepm[:, :], posf[:, :], float(CAP), None, ALU.is_lt)
                sv.inc(vi)
                vi = v.tensor_scalar_mul(slotf[:, :], idxs[:, k:k + 1], float(CAP))
                sv.inc(vi)
                v.wait_ge(sv.sem, sv.n)
                vi = v.tensor_tensor(slotf[:, :], slotf[:, :], posf[:, :], ALU.add)
                sv.inc(vi)
                v.wait_ge(sv.sem, sv.n)
                vi = v.tensor_scalar(col0[:, :], keepm[:, :], -BIG, BIG, ALU.mult, ALU.add)
                sv.inc(vi)
                vi = v.tensor_tensor(slotg[:, :], slotf[:, :], keepm[:, :], ALU.mult)
                sv.inc(vi)
                v.wait_ge(sv.sem, sv.n)
                vi = v.tensor_tensor(slotg[:, :], slotg[:, :], col0[:, :], ALU.add)
                sv.inc(vi)
                v.wait_ge(sv.sem, sv.n)
                vi = v.tensor_copy(slots[:, k:k + 1], slotg[:, :])
                sv.inc(vi)
            SV_SLOT = sv.n

            @block.gpsimd
            def _(g):
                g.wait_ge(sv.sem, SV_SLOT)
                if k == 0:
                    g.wait_ge(lds.sem, LDS_ZERO)
                ldg.dma(g.indirect_dma_start(
                    out=xsort_loc[:, :], out_offset=IOA(slots[:, k:k + 1], 0),
                    in_=x1b[:, k * 1024:(k + 1) * 1024], in_offset=None,
                    bounds_check=NTOK - 1, oob_is_err=False))
            sc_marks.append((st.n, ldg.n))
        LDG_SCAT = ldg.n

        @block.gpsimd
        def _(g):
            g.wait_ge(ldg.sem, LDG_SCAT)
            gi = g.collective_compute("ReduceScatter", ALU.add, replica_groups=[list(range(8))],
                                      ins=[xsort_loc[:, :]], outs=[xsort[:, :]])
            cc.inc(gi)
        CC_RS = cc.n
        if upto <= 4:
            if dbg:
                @block.sync
                def _(s):
                    s.wait_ge(cc.sem, CC_RS)
                    lds.dma(s.dma_start(out=dbg1[:, 0:16], in_=gates[:, :]))
                    lds.dma(s.dma_start(out=dbg2[:, 0:16], in_=idxs[:, :]))
                    lds.dma(s.dma_start(out=dbgxs, in_=xsort.ap()))
                    lds.dma(s.dma_start(out=dbgsl, in_=slots[:, :]))
                    s.wait_ge(lds.sem, lds.n)
            else:
                @block.sync
                def _(s):
                    s.wait_ge(cc.sem, CC_RS)
            return nc

        PH4A_ST = st.n
        PH4A_SS = ss.n
        PH4A_SV = sv.n
        scope4a.close()
        scope3.close()

        scope4b = top.enter_context(ExitStack())
        S4b_ = scope4b.enter_context
        bufT = S4b_(nc.sbuf_tensor("bufT", [128, 8 * 1024], BF16))
        hT = S4b_(nc.sbuf_tensor("hT", [128, 32 * 1024], BF16))
        w2f = S4b_(nc.sbuf_tensor("w2f", [128, 32 * 1024], BF16))
        w1b = [S4b_(nc.sbuf_tensor(f"w1b{i}", [128, 8 * 512], BF16)) for i in range(2)]
        bufS = [S4b_(nc.sbuf_tensor(f"bufS{i}", [128, 1024], BF16)) for i in range(2)]
        b1S = S4b_(nc.sbuf_tensor("b1S", [32, 128], F32))
        b1T = S4b_(nc.sbuf_tensor("b1T", [128, 32], F32))
        b2S = S4b_(nc.sbuf_tensor("b2S", [8, 128], F32))
        b2T = S4b_(nc.sbuf_tensor("b2T", [128, 8], F32))
        ytmp = S4b_(nc.sbuf_tensor("ytmp", [128, 512], F32))
        ytok = S4b_(nc.sbuf_tensor("ytok", [128, 4 * 1024], F32))

        lds_y = []
        prev_h_last = 0
        prev_y_last = 0
        for e in range(EPC):
            scope_p = top.enter_context(ExitStack())
            psBufT = scope_p.enter_context(nc.psum_tensor(f"psBufT{e}", [128, 1024], BF16))
            psTb = scope_p.enter_context(nc.psum_tensor(f"psTb{e}", [128, 512], F32))

            @block.sync
            def _(s):
                s.wait_ge(ldg.sem, LDG_SCAT)
                lds.dma(s.dma_start(out=b1S[:, :], in_=b1_d[e]))
                lds.dma(s.dma_start(out=b2S[:, :], in_=b2_d[e]))
            LDS_B = lds.n

            @block.tensor
            def _(t):
                t.wait_ge(lds.sem, LDS_B)
                if e == 0:
                    t.wait_ge(ss.sem, PH4A_SS)
                    t.wait_ge(sv.sem, PH4A_SV)
                else:
                    t.wait_ge(ss.sem, ss.n)
                    t.wait_ge(sv.sem, sv.n)
                t.transpose(psTb[:, 0:32], b1S[:, :], identf[0:32, 0:32])
                ti = t.transpose(psTb[:, 32:40], b2S[:, :], identf[0:8, 0:8])
                st.inc(ti)
            ST_B = st.n

            @block.vector
            def _(v):
                v.wait_ge(st.sem, ST_B)
                v.tensor_copy(b1T[:, :], psTb[:, 0:32])
                vi = v.tensor_copy(b2T[:, :], psTb[:, 32:40])
                sv.inc(vi)
            SV_B = sv.n

            buf_marks = []
            for i in range(8):
                @block.sync
                def _(s):
                    s.wait_ge(cc.sem, CC_RS)
                    if i >= 2:
                        s.wait_ge(sv.sem, buf_marks[i - 2])
                    lds.dma(s.dma_start(
                        out=bufS[i % 2][:, :],
                        in_=xsort[(e * CAP + i * 128):(e * CAP + (i + 1) * 128), :]))
                LDS_BUF = lds.n

                @block.tensor
                def _(t):
                    t.wait_ge(lds.sem, LDS_BUF)
                    if i == 0:
                        t.wait_ge(sv.sem, SV_B)
                    if i >= 1:
                        t.wait_ge(sv.sem, buf_marks[i - 1])
                    for ccc in range(8):
                        ti = t.transpose(psBufT[:, ccc * 128:(ccc + 1) * 128],
                                         bufS[i % 2][:, ccc * 128:(ccc + 1) * 128],
                                         identb[:, :])
                    st.inc(ti)
                ST_BUF = st.n

                @block.vector
                def _(v):
                    v.wait_ge(st.sem, ST_BUF)
                    for ccc in range(8):
                        vi = v.tensor_copy(bufT[:, ccc * 1024 + i * 128:ccc * 1024 + (i + 1) * 128],
                                           psBufT[:, ccc * 128:(ccc + 1) * 128])
                    sv.inc(vi)
                    buf_marks.append(sv.n)
            SV_BUFT = sv.n

            if dbg and upto == 5 and e == 0 and not _NODUMP:
                @block.gpsimd
                def _(g):
                    g.wait_ge(sv.sem, SV_BUFT)
                    ldg.dma(g.dma_start(out=dbgbt[:, :], in_=bufT[:, 0:2048]))
            scope_p.close()
            scope_h = top.enter_context(ExitStack())
            psH = scope_h.enter_context(nc.psum_tensor(f"psH{e}", [128, 2048], F32))

            h_mm = []
            ss_base_h = ss.n
            for hg in range(8):
                @block.gpsimd
                def _(g):
                    if hg == 0:
                        if e == 0:
                            g.wait_ge(ldg.sem, LDG_SCAT)
                        else:
                            g.wait_ge(st.sem, prev_h_last)
                    if hg >= 2:
                        g.wait_ge(st.sem, h_mm[-1])
                    ldg.dma(g.dma_start(
                        out=w1b[hg % 2][:, :].rearrange("p (cc h) -> p cc h", cc=8),
                        in_=w1_d[e, :, hg * 512:(hg + 1) * 512].rearrange("(cc p) h -> p cc h", p=128)))
                LDG_W1 = ldg.n

                @block.gpsimd
                def _(g):
                    if e > 0 and hg == 0:
                        g.wait_ge(st.sem, prev_y_last)
                    ldg.dma(g.dma_start(
                        out=w2f[:, hg * 4096:(hg + 1) * 4096].rearrange("p (q c) -> p q c", q=4),
                        in_=w2_d[e, hg * 512:(hg + 1) * 512, :].rearrange("(q p) c -> p q c", p=128)))
                LDG_W2F = ldg.n

                for hs4 in range(4):
                    for sti in range(2):
                        bsel = (hs4 * 2 + sti) % 4

                        @block.tensor
                        def _(t):
                            t.wait_ge(ldg.sem, LDG_W1)
                            if hg == 0 and hs4 == 0 and sti == 0:
                                t.wait_ge(sv.sem, SV_BUFT)
                            gi = hg * 8 + hs4 * 2 + sti
                            if gi >= 4:
                                t.wait_ge(ss.sem, ss_base_h + gi - 3)
                            bank = psH[:, bsel * 512:(bsel + 1) * 512]
                            for ccc in range(8):
                                ti = t.matmul(bank,
                                              w1b[hg % 2][:, ccc * 512 + hs4 * 128:ccc * 512 + hs4 * 128 + 128],
                                              bufT[:, ccc * 1024 + sti * 512:ccc * 1024 + sti * 512 + 512],
                                              start=(ccc == 0), stop=(ccc == 7))
                            st.inc(ti)
                            h_mm.append(st.n)

            @block.scalar
            def _(sc):
                for gi in range(64):
                    hg, hs4, sti = gi // 8, (gi % 8) // 2, gi % 2
                    bsel = (hs4 * 2 + sti) % 4
                    sc.wait_ge(st.sem, h_mm[gi])
                    bank = psH[:, bsel * 512:(bsel + 1) * 512]
                    hs = hg * 4 + hs4
                    si = sc.activation(hT[:, hs * 1024 + sti * 512:hs * 1024 + sti * 512 + 512],
                                       bank, AF.Relu,
                                       bias=b1T[:, hs:hs + 1], scale=c_one[:, 0:1])
                    ss.inc(si)
            SS_H = ss.n
            prev_h_last = h_mm[-1]

            if dbg and upto == 5 and e == 0 and not _NODUMP:
                @block.gpsimd
                def _(g):
                    g.wait_ge(ss.sem, SS_H)
                    ldg.dma(g.dma_start(out=dbgh[:, 0:2048], in_=hT[:, 0:2048]))
                    ldg.dma(g.dma_start(out=dbgh[:, 2048:4096], in_=hT[:, 16 * 1024:16 * 1024 + 2048]))
            scope_h.close()
            scope_y = top.enter_context(ExitStack())
            psY = scope_y.enter_context(nc.psum_tensor(f"psY{e}", [128, 4096], F32))

            for sti in range(2):
                y_mm = []

                @block.tensor
                def _(t):
                    t.wait_ge(ldg.sem, LDG_W2F)
                    if sti == 0:
                        t.wait_ge(ss.sem, SS_H)
                    if sti == 1:
                        t.wait_ge(sv.sem, sv.n)
                    for cs in range(8):
                        bank = psY[:, cs * 512:(cs + 1) * 512]
                        for hc in range(32):
                            ti = t.matmul(bank,
                                          w2f[:, hc * 1024 + cs * 128:hc * 1024 + cs * 128 + 128],
                                          hT[:, hc * 1024 + sti * 512:hc * 1024 + sti * 512 + 512],
                                          start=(hc == 0), stop=(hc == 31))
                    st.inc(ti)
                    y_mm.append(st.n)
                Y_MM = y_mm[-1]
                ss_base_y = ss.n
                st_base_y = st.n

                @block.scalar
                def _(sc):
                    for cs in range(8):
                        sc.wait_ge(st.sem, Y_MM)
                        if cs >= 1:
                            sc.wait_ge(st.sem, st_base_y + cs)
                        bank = psY[:, cs * 512:(cs + 1) * 512]
                        si = sc.activation(ytmp[:, :], bank, AF.Identity,
                                           bias=b2T[:, cs:cs + 1], scale=c_one[:, 0:1])
                        ss.inc(si)

                for cs in range(8):
                    @block.tensor
                    def _(t):
                        t.wait_ge(ss.sem, ss_base_y + cs + 1)
                        for tb in range(4):
                            ti = t.transpose(psY[:, cs * 512 + tb * 128:cs * 512 + tb * 128 + 128],
                                             ytmp[:, tb * 128:(tb + 1) * 128],
                                             identf[:, :])
                        st.inc(ti)
                    ST_YT = st.n

                    @block.vector
                    def _(v):
                        v.wait_ge(st.sem, ST_YT)
                        if cs == 0 and lds_y:
                            v.wait_ge(lds.sem, lds_y[-1])
                        for tb in range(4):
                            vi = v.tensor_copy(ytok[:, tb * 1024 + cs * 128:tb * 1024 + cs * 128 + 128],
                                               psY[:, cs * 512 + tb * 128:cs * 512 + tb * 128 + 128])
                        sv.inc(vi)
                SV_YTOK = sv.n

                @block.sync
                def _(s):
                    s.wait_ge(sv.sem, SV_YTOK)
                    for tb in range(4):
                        r0 = e * CAP + sti * 512 + tb * 128
                        lds.dma(s.dma_start(out=cc_y_in[r0:r0 + 128, :],
                                            in_=ytok[:, tb * 1024:(tb + 1) * 1024]))
                lds_y.append(lds.n)

            prev_y_last = st.n
            scope_y.close()

        @block.gpsimd
        def _(g):
            g.wait_ge(lds.sem, lds_y[-1])
            gi = g.collective_compute("AllGather", ALU.bypass, replica_groups=[list(range(8))],
                                      ins=[cc_y_in[:, :]], outs=[cc_y_out[:, :]])
            cc.inc(gi)
        CC_Y = cc.n
        if upto <= 5:
            @block.sync
            def _(s):
                s.wait_ge(cc.sem, CC_Y)
                if dbg:
                    lds.dma(s.dma_start(out=dbgy, in_=cc_y_in.ap()))
                    lds.dma(s.dma_start(out=dbgxs5, in_=xsort.ap()))
                    s.wait_ge(lds.sem, lds.n)
            if dbg:
                @block.gpsimd
                def _(g):
                    g.wait_ge(cc.sem, CC_Y)
                    g.wait_ge(ldg.sem, ldg.n)
            return nc

        PH4B_SS = ss.n
        PH4B_SV = sv.n
        scope4b.close()

        scope4c = top.enter_context(ExitStack())
        S4c_ = scope4c.enter_context
        ygath = [S4c_(nc.sbuf_tensor(f"ygath{i}", [128, 1024], F32)) for i in range(2)]
        x1t = [S4c_(nc.sbuf_tensor(f"x1t{i}", [128, 1024], F32)) for i in range(2)]
        outt = [S4c_(nc.sbuf_tensor(f"outt{i}", [128, 1024], F32)) for i in range(2)]

        yz_marks = []
        so_marks = []
        for k in range(NTT):
            @block.vector
            def _(v):
                if k == 0:
                    v.wait_ge(ss.sem, PH4B_SS)
                    v.wait_ge(lds.sem, lds_y[-1])
                if k >= 2:
                    v.wait_ge(ldg.sem, yz_marks[k - 2][1])
                vi = v.memset(ygath[k % 2][:, :], 0.0)
                sv.inc(vi)
            SV_YZ = sv.n

            @block.gpsimd
            def _(g):
                g.wait_ge(cc.sem, CC_Y)
                g.wait_ge(sv.sem, SV_YZ)
                ldg.dma(g.indirect_dma_start(
                    out=ygath[k % 2][:, :], out_offset=None,
                    in_=cc_y_out[:, :], in_offset=IOA(slots[:, k:k + 1], 0),
                    bounds_check=NTOK - 1, oob_is_err=False))
            LDG_YG = ldg.n
            yz_marks.append((SV_YZ, LDG_YG))

            @block.sync
            def _(s):
                if k >= 2:
                    s.wait_ge(sv.sem, so_marks[k - 2][0])
                lds.dma(s.dma_start(out=x1t[k % 2][:, :],
                                    in_=x1spill[k * 128:(k + 1) * 128, :]))
            LDS_X1T = lds.n

            @block.vector
            def _(v):
                v.wait_ge(ldg.sem, LDG_YG)
                v.wait_ge(lds.sem, LDS_X1T)
                vi = v.tensor_scalar(scr1[:, :], ygath[k % 2][:, :], gates[:, k:k + 1],
                                     None, ALU.mult)
                sv.inc(vi)
                v.wait_ge(sv.sem, sv.n)
                vi = v.tensor_tensor(scr0[:, :], x1t[k % 2][:, :], scr1[:, :], ALU.add)
                sv.inc(vi)
                v.wait_ge(sv.sem, sv.n)
                vi = v.reduce_sum(col0[:, :], scr0[:, :], axis=AX.X)
                sv.inc(vi)
                v.wait_ge(sv.sem, sv.n)
                vi = v.tensor_scalar_mul(col1[:, :], col0[:, :], INV_D)
                sv.inc(vi)
                v.wait_ge(sv.sem, sv.n)
                vi = v.tensor_scalar(scr1[:, :], scr0[:, :], col1[:, 0:1], None, ALU.subtract)
                sv.inc(vi)
            SV_XC2 = sv.n

            @block.scalar
            def _(sc):
                sc.wait_ge(sv.sem, SV_XC2)
                si = sc.activation(scr0[:, :], scr1[:, :], AF.Square,
                                   bias=c_zero[:, 0:1], scale=c_one[:, 0:1],
                                   accum_out=col2[:, :])
                ss.inc(si)
                sc.wait_ge(sv.sem, sv.n + 1)
                si = sc.activation(col3[:, :], col2[:, :], AF.Ln,
                                   bias=c_eps[:, 0:1], scale=c_one[:, 0:1])
                ss.inc(si)
                sc.wait_ge(ss.sem, ss.n)
                si = sc.activation(col3[:, :], col3[:, :], AF.Exp,
                                   bias=c_zero[:, 0:1], scale=c_half[:, 0:1])
                ss.inc(si)
            SS_SQ2 = ss.n - 2

            @block.vector
            def _(v):
                v.wait_ge(ss.sem, SS_SQ2)
                vi = v.tensor_scalar_mul(col2[:, :], col2[:, :], INV_D)
                sv.inc(vi)
            SS_RSTD2 = SS_SQ2 + 2

            @block.vector
            def _(v):
                v.wait_ge(ss.sem, SS_RSTD2)
                if k >= 2:
                    v.wait_ge(so.sem, so_marks[k - 2][1])
                vi = v.tensor_scalar(outt[k % 2][:, :], scr1[:, :], col3[:, 0:1], None, ALU.mult)
                sv.inc(vi)
            SV_OUT = sv.n

            @block.sync
            def _(s):
                s.wait_ge(sv.sem, SV_OUT)
                so.dma(s.dma_start(out=out_d[k * 128:(k + 1) * 128, :], in_=outt[k % 2][:, :]))
            so_marks.append((SV_OUT, so.n))

        @block.sync
        def _(s):
            s.wait_ge(so.sem, so_marks[-1][1])

        scope4c.close()

    return nc


def kernel(L0_attn_x, L0_attn_w_qkv, L0_attn_w_proj, L0_attn_b_proj,
           L0_ffn_Wg, L0_ffn_bg, L0_ffn_W1, L0_ffn_b1, L0_ffn_W2, L0_ffn_b2):
    x = np.asarray(L0_attn_x, dtype=np.float32)
    w_qkv = np.asarray(L0_attn_w_qkv, dtype=np.float32)
    w_proj = np.ascontiguousarray(np.asarray(L0_attn_w_proj, dtype=np.float32))
    b_proj = np.asarray(L0_attn_b_proj, dtype=np.float32)
    Wg = np.ascontiguousarray(np.asarray(L0_ffn_Wg, dtype=np.float32))
    bg = np.asarray(L0_ffn_bg, dtype=np.float32)
    W1 = np.asarray(L0_ffn_W1, dtype=np.float32)
    b1 = np.asarray(L0_ffn_b1, dtype=np.float32)
    W2 = np.asarray(L0_ffn_W2, dtype=np.float32)
    b2 = np.asarray(L0_ffn_b2, dtype=np.float32)

    xt = [np.ascontiguousarray(x[b].T) for b in range(B)]
    in_maps = []
    for c in range(8):
        b = c // 4
        g = c % 4
        h0 = 4 * g
        wq = w_qkv[:, 64 * h0: 64 * h0 + 256]
        wk = w_qkv[:, D + 64 * h0: D + 64 * h0 + 256]
        wv = w_qkv[:, 2 * D + 64 * h0: 2 * D + 64 * h0 + 256]
        wsel = np.ascontiguousarray(
            np.concatenate([wq[:, 0:128], wq[:, 128:256], wk[:, 0:128], wk[:, 128:256], wv], axis=1))
        toff = np.full((128, 32), g * TOK, dtype=np.int32)
        in_maps.append({
            "xt": xt[b],
            "xres": np.ascontiguousarray(x[b, g * TOK:(g + 1) * TOK, :]),
            "wqkv_sel": wsel,
            "wproj": w_proj,
            "bprojr": np.ascontiguousarray(b_proj.reshape(8, 128)),
            "wg": Wg,
            "bg1": np.ascontiguousarray(bg.reshape(1, NE)),
            "w1": np.ascontiguousarray(W1[2 * c:2 * c + 2]),
            "b1r": np.ascontiguousarray(b1[2 * c:2 * c + 2].reshape(EPC, 32, 128)),
            "w2": np.ascontiguousarray(W2[2 * c:2 * c + 2]),
            "b2r": np.ascontiguousarray(b2[2 * c:2 * c + 2].reshape(EPC, 8, 128)),
            "rankmask": (np.arange(8) < c).astype(np.float32).reshape(8, 1),
            "toff": toff,
        })

    nc = build_program()
    res = run_bass_kernel_spmd(nc, in_maps, core_ids=list(range(8)))
    parts = [res.results[c]["out"] for c in range(8)]
    full = np.concatenate(parts, axis=0).reshape(B, N, D)
    return full.astype(np.float32)


# version 13
# speedup vs baseline: 1.0129x; 1.0129x over previous
import numpy as np
import concourse.bass as bass
import concourse.mybir as mybir
from concourse.bass_utils import run_bass_kernel_spmd
from contextlib import ExitStack

F32 = mybir.dt.float32
BF16 = mybir.dt.bfloat16
I32 = mybir.dt.int32
AF = mybir.ActivationFunctionType
ALU = mybir.AluOpType
AX = mybir.AxisListType
IOA = bass.IndirectOffsetOnAxis

B = 2
N = 8192
D = 1024
HD = 64
TOK = 2048
NTOK = B * N
NE = 16
CAP = 1024
HID = 4096
EPC = 2
NQT = N // 512
NKB = N // 128
SCALE = 0.125
CSHIFT = 16.0
EPS = 1e-5
BIG = 1.0e9
NTT = TOK // 128
INV_D = 1.0 / D


class Ctr:
    def __init__(self, sem):
        self.sem = sem
        self.n = 0

    def inc(self, inst, k=1):
        inst.then_inc(self.sem, k)
        self.n += k
        return self.n

    def dma(self, inst):
        inst.then_inc(self.sem, 16)
        self.n += 16
        return self.n


_LDW_PATCHED = False


def _enable_ldw_opt():
    global _LDW_PATCHED
    if _LDW_PATCHED:
        return
    import concourse.bass_utils as _bu
    _real = _bu.run_command

    def _patched(cmd, **kw):
        cmd = [c.replace("--enable-ldw-opt=false", "--enable-ldw-opt=true")
               if isinstance(c, str) else c for c in cmd]
        return _real(cmd, **kw)

    _bu.run_command = _patched
    _LDW_PATCHED = True


def build_program(upto=99, dbg=False, _NODUMP=False):
    _enable_ldw_opt()
    nc = bass.Bass("TRN2", target_bir_lowering=False, debug=False)

    xt_d = nc.dram_tensor("xt", [D, N], F32, kind="ExternalInput").ap()
    xres_d = nc.dram_tensor("xres", [TOK, D], F32, kind="ExternalInput").ap()
    wqkv_d = nc.dram_tensor("wqkv_sel", [D, 768], F32, kind="ExternalInput").ap()
    wproj_d = nc.dram_tensor("wproj", [D, D], F32, kind="ExternalInput").ap()
    bproj_d = nc.dram_tensor("bprojr", [8, 128], F32, kind="ExternalInput").ap()
    wg_d = nc.dram_tensor("wg", [D, NE], F32, kind="ExternalInput").ap()
    bg_d = nc.dram_tensor("bg1", [1, NE], F32, kind="ExternalInput").ap()
    w1_d = nc.dram_tensor("w1", [EPC, D, HID], F32, kind="ExternalInput").ap()
    b1_d = nc.dram_tensor("b1r", [EPC, 32, 128], F32, kind="ExternalInput").ap()
    w2_d = nc.dram_tensor("w2", [EPC, HID, D], F32, kind="ExternalInput").ap()
    b2_d = nc.dram_tensor("b2r", [EPC, 8, 128], F32, kind="ExternalInput").ap()
    rankmask_d = nc.dram_tensor("rankmask", [8, 1], F32, kind="ExternalInput").ap()
    toff_d = nc.dram_tensor("toff", [128, 32], I32, kind="ExternalInput").ap()
    out_d = nc.dram_tensor("out", [TOK, D], F32, kind="ExternalOutput").ap()

    if dbg:
        dbg1 = nc.dram_tensor("dbg1", [128, 2048], F32, kind="ExternalOutput").ap()
        dbg2 = nc.dram_tensor("dbg2", [128, 2048], F32, kind="ExternalOutput").ap()
    cc_att_in = nc.dram_tensor("cc_att_in", [256, N], BF16)
    cc_att_out = nc.dram_tensor("cc_att_out", [D, N], BF16)
    x1spill = nc.dram_tensor("x1spill", [TOK, D], F32,
                              kind="ExternalOutput" if (dbg and upto == 3) else "Internal")
    xsort_loc = nc.dram_tensor("xsort_loc", [NTOK, D], BF16)
    xsort = nc.dram_tensor("xsort", [TOK, D], BF16)
    if dbg and upto == 4:
        dbgxs = nc.dram_tensor("dbgxs", [TOK, D], BF16, kind="ExternalOutput").ap()
        dbgsl = nc.dram_tensor("dbgsl", [128, NTT], I32, kind="ExternalOutput").ap()
    cc_cnt_in = nc.dram_tensor("cc_cnt_in", [1, NE], F32)
    cc_cnt_out = nc.dram_tensor("cc_cnt_out", [8, NE], F32, addr_space="Shared")
    cc_y_in = nc.dram_tensor("cc_y_in", [TOK, D], BF16)
    if dbg and upto == 5:
        dbgy = nc.dram_tensor("dbgy", [TOK, D], F32, kind="ExternalOutput").ap()
        dbgxs5 = nc.dram_tensor("dbgxs5", [TOK, D], BF16, kind="ExternalOutput").ap()
        dbgh = nc.dram_tensor("dbgh", [128, 4096], BF16, kind="ExternalOutput").ap()
        dbgbt = nc.dram_tensor("dbgbt", [128, 2048], BF16, kind="ExternalOutput").ap()
    cc_y_out = nc.dram_tensor("cc_y_out", [NTOK, D], BF16, addr_space="Shared")

    top = ExitStack()
    with top:
        T_ = top.enter_context

        ones128 = T_(nc.sbuf_tensor("ones128", [128, 128], F32))
        identb = T_(nc.sbuf_tensor("identb", [128, 128], BF16))
        identf = T_(nc.sbuf_tensor("identf", [128, 128], F32))
        triu = T_(nc.sbuf_tensor("triu", [128, 128], F32))
        iota16 = T_(nc.sbuf_tensor("iota16", [128, NE], F32))
        onescb = T_(nc.sbuf_tensor("onescb", [128, 1], BF16))
        onesrb = T_(nc.sbuf_tensor("onesrb", [1, 128], BF16))
        c_one = T_(nc.sbuf_tensor("c_one", [128, 1], F32))
        c_zero = T_(nc.sbuf_tensor("c_zero", [128, 1], F32))
        c_exps = T_(nc.sbuf_tensor("c_exps", [128, 1], F32))
        c_expb = T_(nc.sbuf_tensor("c_expb", [128, 1], F32))
        c_half = T_(nc.sbuf_tensor("c_half", [128, 1], F32))
        c_eps = T_(nc.sbuf_tensor("c_eps", [128, 1], F32))
        c_big = T_(nc.sbuf_tensor("c_big", [128, 1], F32))
        zerot = T_(nc.sbuf_tensor("zerot", [128, 1024], BF16))
        scr0 = T_(nc.sbuf_tensor("scr0", [128, 1024], F32))
        scr1 = T_(nc.sbuf_tensor("scr1", [128, 1024], F32))
        col0 = T_(nc.sbuf_tensor("col0", [128, 1], F32))
        col1 = T_(nc.sbuf_tensor("col1", [128, 1], F32))
        col2 = T_(nc.sbuf_tensor("col2", [128, 1], F32))
        col3 = T_(nc.sbuf_tensor("col3", [128, 1], F32))
        Lg = T_(nc.sbuf_tensor("Lg", [128, NE], F32))
        expl = T_(nc.sbuf_tensor("expl", [128, NE], F32))
        ohw = T_(nc.sbuf_tensor("ohw", [128, NE], F32))
        mrow = T_(nc.sbuf_tensor("mrow", [1, NE], F32))
        ohs = T_(nc.sbuf_tensor("ohs", [128, NTT * NE], F32))
        idxs = T_(nc.sbuf_tensor("idxs", [128, NTT], F32))
        gates = T_(nc.sbuf_tensor("gates", [128, NTT], F32))
        slots = T_(nc.sbuf_tensor("slots", [128, NTT], I32))
        posf = T_(nc.sbuf_tensor("posf", [128, 1], F32))
        keepm = T_(nc.sbuf_tensor("keepm", [128, 1], F32))
        slotf = T_(nc.sbuf_tensor("slotf", [128, 1], F32))
        slotg = T_(nc.sbuf_tensor("slotg", [128, 1], F32))
        basetile = T_(nc.sbuf_tensor("basetile", [1, NTT * NE], F32))
        basecore = T_(nc.sbuf_tensor("basecore", [1, NE], F32))
        cnts_all = T_(nc.sbuf_tensor("cnts_all", [8, NE], F32))
        cntrow = T_(nc.sbuf_tensor("cntrow", [1, NE], F32))
        rankm = T_(nc.sbuf_tensor("rankm", [8, 1], F32))
        bgb = T_(nc.sbuf_tensor("bgb", [1, NE], F32))
        wgb = T_(nc.sbuf_tensor("wgb", [128, 8 * NE], F32))
        bprojT = T_(nc.sbuf_tensor("bprojT", [128, 8], F32))
        bprojS = T_(nc.sbuf_tensor("bprojS", [8, 128], F32))

        lds = Ctr(T_(nc.semaphore("lds")))
        ldg = Ctr(T_(nc.semaphore("ldg")))
        st = Ctr(T_(nc.semaphore("st")))
        ss = Ctr(T_(nc.semaphore("ss")))
        sv = Ctr(T_(nc.semaphore("sv")))
        sg = Ctr(T_(nc.semaphore("sg")))
        cc = Ctr(T_(nc.semaphore("cc")))
        sqk = Ctr(T_(nc.semaphore("sqk")))
        sxp = Ctr(T_(nc.semaphore("sxp")))
        so = Ctr(T_(nc.semaphore("so")))

        block = T_(nc.Block())

        @block.vector
        def _(v):
            v.memset(ones128[:, :], 1.0)
            v.memset(onescb[:, :], 1.0)
            v.memset(onesrb[:, :], 1.0)
            v.memset(c_one[:, :], 1.0)
            v.memset(c_zero[:, :], 0.0)
            v.memset(c_exps[:, :], SCALE)
            v.memset(c_expb[:, :], -CSHIFT)
            v.memset(c_half[:, :], -0.5)
            v.memset(c_eps[:, :], EPS)
            v.memset(c_big[:, :], BIG)
            v.memset(zerot[:, :], 0.0)
            vi = v.memset(basetile[:, 0:NE], 0.0)
            sv.inc(vi)
        SV_CONST = sv.n

        @block.gpsimd
        def _(g):
            g.wait_ge(sv.sem, SV_CONST)
            g.iota(iota16[:, :], pattern=[[1, NE]], base=0, channel_multiplier=0,
                   allow_small_or_imprecise_dtypes=True)
            g.affine_select(identb[:, :], ones128[:, :], pattern=[[-1, 128]],
                            compare_op=ALU.is_equal, fill=0.0, base=0, channel_multiplier=1)
            g.affine_select(identf[:, :], ones128[:, :], pattern=[[-1, 128]],
                            compare_op=ALU.is_equal, fill=0.0, base=0, channel_multiplier=1)
            gi = g.affine_select(triu[:, :], ones128[:, :], pattern=[[1, 128]],
                                 compare_op=ALU.is_gt, fill=0.0, base=0, channel_multiplier=-1)
            sg.inc(gi)
        SG_CONST = sg.n

        @block.sync
        def _(s):
            lds.dma(s.dma_start(out=wgb[:, :].rearrange("p (cc e) -> p cc e", cc=8),
                                in_=wg_d.rearrange("(cc p) e -> p cc e", p=128)))
            lds.dma(s.dma_start(out=bgb[:, :], in_=bg_d))

        @block.sync
        def _(s):
            lds.dma(s.dma_start(out=bprojS[:, :], in_=bproj_d))
            lds.dma(s.dma_start(out=rankm[:, :], in_=rankmask_d))
        LDS_W0 = lds.n

        @block.sync
        def _(s):
            s.wait_ge(sv.sem, SV_CONST)
            for i in range(NTOK // 128):
                lds.dma(s.dma_start(out=xsort_loc[128 * i:128 * (i + 1), :], in_=zerot[:, :]))
        LDS_ZERO = lds.n

        scopeA = top.enter_context(ExitStack())
        A_ = scopeA.enter_context
        kTp = [A_(nc.sbuf_tensor(f"kTp{p}", [128, N], BF16)) for p in range(2)]
        qTp = [A_(nc.sbuf_tensor(f"qTp{p}", [128, N], BF16)) for p in range(2)]
        vh = A_(nc.sbuf_tensor("vh", [128, NKB * 4 * 65], BF16))
        rowt = A_(nc.sbuf_tensor("rowt", [128, 1024], F32))
        obuf2 = A_(nc.sbuf_tensor("obuf2", [64, 512], BF16))
        ocopyA = A_(nc.sbuf_tensor("ocopyA", [128, 512], F32))
        ocopyB = A_(nc.sbuf_tensor("ocopyB", [128, 512], F32))
        erow = A_(nc.sbuf_tensor("erow", [128, 512], F32))
        erow2 = A_(nc.sbuf_tensor("erow2", [128, 512], F32))

        scope1 = top.enter_context(ExitStack())
        S1_ = scope1.enter_context
        wqkvb = S1_(nc.sbuf_tensor("wqkvb", [128, 8 * 768], BF16))
        xtq = [S1_(nc.sbuf_tensor(f"xtq{i}", [128, 8 * 512], BF16)) for i in range(2)]
        psQ = S1_(nc.psum_tensor("psQ", [128, 2048], F32))
        psV = S1_(nc.psum_tensor("psV", [128, 1024], F32))

        @block.gpsimd
        def _(g):
            ldg.dma(g.dma_start(out=wqkvb[:, :].rearrange("p (cc o) -> p cc o", cc=8),
                                in_=wqkv_d.rearrange("(cc p) o -> p cc o", p=128)))
        LDG_WQKV = ldg.n

        @block.vector
        def _(v):
            vi = v.memset(vh[:, :], 1.0)
            sv.inc(vi)

        ldg_x = []
        qk_done = []
        v_done = []
        ev_qk = []
        ev_v = []
        for tt in range(16):
            @block.gpsimd
            def _(g):
                if tt >= 2:
                    g.wait_ge(st.sem, v_done[tt - 2])
                ldg.dma(g.dma_start(
                    out=xtq[tt % 2][:, :].rearrange("p (cc t) -> p cc t", cc=8),
                    in_=xt_d[:, 512 * tt:512 * (tt + 1)].rearrange("(cc p) t -> p cc t", p=128)))
            ldg_x.append(ldg.n)

            @block.tensor
            def _(t):
                t.wait_ge(ldg.sem, ldg_x[tt])
                if tt == 0:
                    t.wait_ge(ldg.sem, LDG_WQKV)
                    t.wait_ge(sg.sem, SG_CONST)
                    t.wait_ge(sv.sem, SV_CONST)
                if tt >= 1:
                    t.wait_ge(ss.sem, ev_qk[tt - 1])
                for ccc in range(8):
                    xs = xtq[tt % 2][:, ccc * 512:(ccc + 1) * 512]
                    fl = dict(start=(ccc == 0), stop=(ccc == 7))
                    t.matmul(psQ[:, 0:512], wqkvb[:, ccc * 768 + 0:ccc * 768 + 128], xs, **fl)
                    t.matmul(psQ[:, 512:1024], wqkvb[:, ccc * 768 + 128:ccc * 768 + 256], xs, **fl)
                    t.matmul(psQ[:, 1024:1536], wqkvb[:, ccc * 768 + 256:ccc * 768 + 384], xs, **fl)
                    ti = t.matmul(psQ[:, 1536:2048], wqkvb[:, ccc * 768 + 384:ccc * 768 + 512], xs, **fl)
                st.inc(ti)
                qk_done.append(st.n)
                if tt >= 1:
                    t.wait_ge(sv.sem, ev_v[tt - 1])
                for vsub in range(4):
                    vps = psV[:, vsub * 256:(vsub + 1) * 256]
                    for ccc in range(8):
                        ti = t.matmul(vps,
                                      xtq[tt % 2][:, ccc * 512 + vsub * 128:ccc * 512 + vsub * 128 + 128],
                                      wqkvb[:, ccc * 768 + 512:ccc * 768 + 768],
                                      start=(ccc == 0), stop=(ccc == 7))
                st.inc(ti)
                v_done.append(st.n)

            @block.scalar
            def _(sc):
                sc.wait_ge(st.sem, qk_done[tt])
                qc = slice(tt * 512, tt * 512 + 512)
                sc.activation(qTp[0][:, qc], psQ[:, 0:512], AF.Identity,
                              bias=c_zero[:, 0:1], scale=c_one[:, 0:1])
                sc.activation(qTp[1][:, qc], psQ[:, 512:1024], AF.Identity,
                              bias=c_zero[:, 0:1], scale=c_one[:, 0:1])
                sc.activation(kTp[0][:, qc], psQ[:, 1024:1536], AF.Identity,
                              bias=c_zero[:, 0:1], scale=c_one[:, 0:1])
                si = sc.activation(kTp[1][:, qc], psQ[:, 1536:2048], AF.Identity,
                                   bias=c_zero[:, 0:1], scale=c_one[:, 0:1])
                ss.inc(si)
                ev_qk.append(ss.n)

            @block.vector
            def _(v):
                v.wait_ge(st.sem, v_done[tt])
                for vsub in range(4):
                    kb = tt * 4 + vsub
                    vps = psV[:, vsub * 256:(vsub + 1) * 256]
                    for h in range(4):
                        vi = v.tensor_copy(vh[:, (kb * 4 + h) * 65:(kb * 4 + h) * 65 + HD],
                                           vps[:, h * HD:(h + 1) * HD])
                sv.inc(vi)
                ev_v.append(sv.n)

        if upto <= 1:
            if dbg:
                @block.gpsimd
                def _(g):
                    g.wait_ge(ss.sem, ev_qk[-1])
                    g.wait_ge(sv.sem, ev_v[-1])
                    ldg.dma(g.dma_start(out=dbg1[:, 0:1024], in_=qTp[0][:, 0:1024]))
                    ldg.dma(g.dma_start(out=dbg1[:, 1024:2048], in_=kTp[1][:, 0:1024]))
                    ldg.dma(g.dma_start(out=dbg2[:, 0:1024], in_=vh[:, 0:1024]))
                    ldg.dma(g.dma_start(out=dbg2[:, 1024:2048], in_=qTp[1][:, 0:1024]))
                    g.wait_ge(ldg.sem, ldg.n)
            else:
                @block.sync
                def _(s):
                    s.wait_ge(ss.sem, ev_qk[-1])
                    s.wait_ge(sv.sem, ev_v[-1])
            return nc
        scope1.close()

        oall = [A_(nc.sbuf_tensor(f"oall{p}", [128, N], BF16)) for p in range(2)]
        pbuf = [A_(nc.sbuf_tensor(f"pbuf{i}", [128, 1024], BF16)) for i in range(2)]
        scope2 = top.enter_context(ExitStack())
        S2_ = scope2.enter_context
        psS = [S2_(nc.psum_tensor(f"psS{i}", [128, 1024], F32)) for i in range(2)]
        psOA = S2_(nc.psum_tensor("psOA", [128, 512], F32))
        psOB = S2_(nc.psum_tensor("psOB", [128, 512], F32))
        psEpA = S2_(nc.psum_tensor("psEpA", [128, 512], F32))
        psEpB = S2_(nc.psum_tensor("psEpB", [128, 512], F32))

        PH1_SS = ss.n
        PH1_SV = sv.n
        NIT = 2 * NQT * NKB

        pv_marks = {}
        ob_marks = []
        omul_marks = {}
        ep_prev_ocopy = {}
        rc_marks = {}
        bc_marks = {}
        ecp_marks = {}
        ep_marks = {}

        def emit_qk(t, j, skip_wait=False):
            p, r = j // (NQT * NKB), j % (NQT * NKB)
            qt, kb = r // NKB, r % NKB
            if j >= 2 and not skip_wait:
                t.wait_ge(sxp.sem, j - 1)
            if j == 0:
                t.wait_ge(ss.sem, PH1_SS)
                t.wait_ge(sv.sem, PH1_SV)
            kc = slice(kb * 128, kb * 128 + 128)
            qc = slice(qt * 512, qt * 512 + 512)
            t.matmul(psS[j % 2][:, 0:512], kTp[p][0:64, kc], qTp[p][0:64, qc],
                     start=True, stop=True, tile_position=(0, 0))
            ti = t.matmul(psS[j % 2][:, 512:1024], kTp[p][64:128, kc], qTp[p][64:128, qc],
                          start=True, stop=True, tile_position=(64, 0))
            sqk.inc(ti)

        def emit_pv(t, j):
            p, r = j // (NQT * NKB), j % (NQT * NKB)
            qt, kb = r // NKB, r % NKB
            t.wait_ge(sxp.sem, j + 1)
            if kb == 0 and (p, qt) != (0, 0):
                prev = (p, qt - 1) if qt > 0 else (p - 1, NQT - 1)
                t.wait_ge(sv.sem, ep_marks[prev])
            pb = pbuf[j % 2]
            first = (kb == 0)
            last = (kb == NKB - 1)
            t.matmul(psOA[0:65, :], vh[:, (kb * 4 + 2 * p) * 65:(kb * 4 + 2 * p) * 65 + 65],
                     pb[:, 0:512], start=first, stop=last)
            ti = t.matmul(psOB[0:65, :], vh[:, (kb * 4 + 2 * p + 1) * 65:(kb * 4 + 2 * p + 1) * 65 + 65],
                          pb[:, 512:1024], start=first, stop=last)
            if last:
                st.inc(ti)
                pv_marks[(p, qt)] = st.n

        def emit_ep_recip(pe, qe):
            @block.vector
            def _(v):
                v.wait_ge(st.sem, pv_marks[(pe, qe)])
                if (pe, qe) in ep_prev_ocopy:
                    v.wait_ge(sv.sem, ep_prev_ocopy[(pe, qe)])
                v.tensor_copy(ocopyA[0:65, :], psOA[0:65, :])
                vi = v.tensor_copy(ocopyB[0:65, :], psOB[0:65, :])
                sv.inc(vi)
                ep_marks[(pe, qe)] = sv.n
                v.wait_ge(sv.sem, sv.n)
                v.reciprocal(rowt[64:65, 0:512], ocopyA[64:65, :])
                vi = v.reciprocal(rowt[64:65, 512:1024], ocopyB[64:65, :])
                sv.inc(vi)
                rc_marks[(pe, qe)] = sv.n

        def emit_ep_bcast(t, pe, qe):
            t.wait_ge(sv.sem, rc_marks[(pe, qe)])
            if (pe, qe) != (0, 0):
                prev = (pe, qe - 1) if qe > 0 else (pe - 1, NQT - 1)
                t.wait_ge(sv.sem, ecp_marks[prev])
            t.matmul(psEpA[0:64, :], ones128[64:65, 0:64], rowt[64:65, 0:512],
                     start=True, stop=True, tile_position=(64, 0))
            ti = t.matmul(psEpB[0:64, :], ones128[64:65, 0:64], rowt[64:65, 512:1024],
                          start=True, stop=True, tile_position=(64, 0))
            st.inc(ti)
            bc_marks[(pe, qe)] = st.n

        def emit_ep_mult(pe, qe):
            @block.vector
            def _(v):
                v.wait_ge(st.sem, bc_marks[(pe, qe)])
                if ob_marks:
                    v.wait_ge(lds.sem, ob_marks[-1])
                v.tensor_copy(erow[0:64, :], psEpA[0:64, :])
                vi = v.tensor_copy(erow2[0:64, :], psEpB[0:64, :])
                sv.inc(vi)
                ecp_marks[(pe, qe)] = sv.n
                v.wait_ge(sv.sem, ecp_marks[(pe, qe)])
                qc = slice(qe * 512, qe * 512 + 512)
                v.tensor_tensor(oall[pe][0:64, qc], ocopyA[0:64, :], erow[0:64, :], ALU.mult)
                vi = v.tensor_tensor(obuf2[0:64, :], ocopyB[0:64, :], erow2[0:64, :], ALU.mult)
                sv.inc(vi)
                omul_marks[(pe, qe)] = sv.n
                nxt = (pe, qe + 1) if qe + 1 < NQT else (pe + 1, 0)
                ep_prev_ocopy[nxt] = sv.n

            @block.sync
            def _(s):
                s.wait_ge(sv.sem, omul_marks[(pe, qe)])
                qc = slice(qe * 512, qe * 512 + 512)
                lds.dma(s.dma_start(out=oall[pe][64:128, qc], in_=obuf2[0:64, :]))
                ob_marks.append(lds.n)

        for p in range(2):
            for qt in range(NQT):
                j0 = (p * NQT + qt) * NKB
                prev = None if (p, qt) == (0, 0) else ((p, qt - 1) if qt > 0 else (p - 1, NQT - 1))

                @block.tensor
                def _(t):
                    emit_qk(t, j0)
                    if j0 >= 1:
                        emit_pv(t, j0 - 1)
                if prev is not None:
                    emit_ep_recip(*prev)

                    @block.tensor
                    def _(t):
                        emit_ep_bcast(t, *prev)
                    emit_ep_mult(*prev)

                @block.tensor
                def _(t):
                    for kb in range(1, NKB):
                        emit_qk(t, j0 + kb)
                        emit_pv(t, j0 + kb - 1)

                @block.scalar
                def _(sc):
                    for kb in range(NKB):
                        j = j0 + kb
                        sc.wait_ge(sqk.sem, j + 1)
                        si = sc.activation(pbuf[j % 2][:, :], psS[j % 2][:, 0:1024], AF.Exp,
                                           bias=c_expb[:, 0:1], scale=c_exps[:, 0:1])
                        sxp.inc(si)

        @block.tensor
        def _(t):
            emit_pv(t, NIT - 1)
        emit_ep_recip(1, NQT - 1)

        @block.tensor
        def _(t):
            emit_ep_bcast(t, 1, NQT - 1)
        emit_ep_mult(1, NQT - 1)
        EP_FINAL = omul_marks[(1, NQT - 1)]

        @block.sync
        def _(s):
            s.wait_ge(sv.sem, EP_FINAL)
            lds.dma(s.dma_start(out=cc_att_in[0:128, :], in_=oall[0][:, :]))
            lds.dma(s.dma_start(out=cc_att_in[128:256, :], in_=oall[1][:, :]))
        LDS_OALL = lds.n

        @block.gpsimd
        def _(g):
            g.wait_ge(lds.sem, LDS_OALL)
            gi = g.collective_compute(
                "AllGather", ALU.bypass, replica_groups=[[0, 1, 2, 3], [4, 5, 6, 7]],
                ins=[cc_att_in[:, :]], outs=[cc_att_out[:, :]])
            cc.inc(gi)
        CC_ATT = cc.n

        if upto <= 2:
            if dbg:
                @block.gpsimd
                def _(g):
                    g.wait_ge(cc.sem, CC_ATT)
                    ldg.dma(g.dma_start(out=dbg1[:, 0:2048], in_=oall[0][:, 0:2048]))
                    ldg.dma(g.dma_start(out=dbg2[:, 0:2048], in_=oall[1][:, 0:2048]))
                    g.wait_ge(ldg.sem, ldg.n)
            else:
                @block.sync
                def _(s):
                    s.wait_ge(cc.sem, CC_ATT)
            return nc

        PH2_ST = st.n
        PH2_SS = ss.n
        scope2.close()
        scopeA.close()

        scope3 = top.enter_context(ExitStack())
        S3_ = scope3.enter_context
        wprojb = S3_(nc.sbuf_tensor("wprojb", [128, 8 * 1024], BF16))
        ccTt = S3_(nc.sbuf_tensor("ccTt", [128, 8 * 512], BF16))
        aprojT = S3_(nc.sbuf_tensor("aprojT", [128, 8 * 512], F32))
        xres_t = [S3_(nc.sbuf_tensor(f"xres{i}", [128, 1024], F32)) for i in range(2)]
        x1b = S3_(nc.sbuf_tensor("x1b", [128, NTT * 1024], BF16))
        x1Tt = S3_(nc.sbuf_tensor("x1Tt", [128, 1024], F32))
        xf_t = [S3_(nc.sbuf_tensor(f"xf_t{i}", [128, 1024], F32)) for i in range(2)]
        idxg = S3_(nc.sbuf_tensor("idxg", [128, 32], I32))
        toffb = S3_(nc.sbuf_tensor("toffb", [128, 32], I32))
        scope3m = top.enter_context(ExitStack())
        S3m_ = scope3m.enter_context
        psPr = S3m_(nc.psum_tensor("psPr", [128, 2048], F32))
        psTr = S3m_(nc.psum_tensor("psTr", [128, 2048], F32))

        @block.gpsimd
        def _(g):
            g.wait_ge(lds.sem, LDS_OALL)
            ldg.dma(g.dma_start(out=toffb[:, :], in_=toff_d))
            for ccc in range(8):
                for tt4 in range(4):
                    gi = g.iota(idxg[:, ccc * 4 + tt4:ccc * 4 + tt4 + 1], pattern=[[0, 1]],
                                base=ccc * 128 * N + tt4 * 512, channel_multiplier=N)
            sg.inc(gi)
            ldg.dma(g.dma_start(out=wprojb[:, :].rearrange("p (cc o) -> p cc o", cc=8),
                                in_=wproj_d.rearrange("(cc p) o -> p cc o", p=128)))
        SG_IDX = sg.n
        LDG_WPROJ = ldg.n

        @block.vector
        def _(v):
            v.wait_ge(sg.sem, SG_IDX)
            v.wait_ge(ldg.sem, LDG_WPROJ)
            vi = v.tensor_tensor(idxg[:, :], idxg[:, :], toffb[:, :], ALU.add)
            sv.inc(vi)
        SV_IDX = sv.n

        @block.tensor
        def _(t):
            t.wait_ge(lds.sem, LDS_W0)
            t.wait_ge(ss.sem, PH2_SS)
            t.wait_ge(sv.sem, EP_FINAL)
            t.wait_ge(sg.sem, SG_CONST)
            ti = t.transpose(psTr[:, 0:8], bprojS[0:8, :], identf[0:8, 0:8])
            st.inc(ti)
        ST_BPJ = st.n

        @block.vector
        def _(v):
            v.wait_ge(st.sem, ST_BPJ)
            vi = v.tensor_copy(bprojT[:, 0:8], psTr[:, 0:8])
            sv.inc(vi)
        SV_BPJ = sv.n

        x1_done = []
        ldg_cct = []
        pr_evac = []
        prmm_last = []
        for tt4 in range(4):
            @block.gpsimd
            def _(g):
                g.wait_ge(cc.sem, CC_ATT)
                g.wait_ge(sv.sem, SV_IDX)
                if tt4 >= 1:
                    g.wait_ge(st.sem, prmm_last[tt4 - 1])
                for ccc in range(8):
                    ldg.dma(g.indirect_dma_start(
                        out=ccTt[:, ccc * 512:(ccc + 1) * 512], out_offset=None,
                        in_=cc_att_out[:, :],
                        in_offset=IOA(idxg[:, ccc * 4 + tt4:ccc * 4 + tt4 + 1], 1)))
            ldg_cct.append(ldg.n)

            ss_base = ss.n
            prmm_marks = []
            for cs in range(8):
                @block.tensor
                def _(t):
                    t.wait_ge(ldg.sem, ldg_cct[tt4])
                    if tt4 == 0 and cs == 0:
                        t.wait_ge(sv.sem, SV_BPJ)
                    gi = tt4 * 8 + cs
                    if gi >= 4:
                        t.wait_ge(ss.sem, ss_base + cs - 3 if cs >= 4 else pr_evac[gi - 4])
                    bank = psPr[:, (cs % 4) * 512:(cs % 4) * 512 + 512]
                    for ccc in range(8):
                        ti = t.matmul(bank,
                                      wprojb[:, ccc * 1024 + cs * 128:ccc * 1024 + cs * 128 + 128],
                                      ccTt[:, ccc * 512:(ccc + 1) * 512],
                                      start=(ccc == 0), stop=(ccc == 7))
                    st.inc(ti)
                prmm_marks.append(st.n)
                if cs == 7:
                    prmm_last.append(st.n)

            @block.scalar
            def _(sc):
                for cs in range(8):
                    sc.wait_ge(st.sem, prmm_marks[cs])
                    bank = psPr[:, (cs % 4) * 512:(cs % 4) * 512 + 512]
                    si = sc.activation(aprojT[:, cs * 512:(cs + 1) * 512], bank, AF.Identity,
                                       bias=bprojT[:, cs:cs + 1], scale=c_one[:, 0:1])
                    ss.inc(si)
                    pr_evac.append(ss.n)

            for tsub in range(4):
                ti128 = tt4 * 4 + tsub

                @block.sync
                def _(s):
                    if ti128 >= 2:
                        s.wait_ge(sv.sem, x1_done[ti128 - 2])
                    lds.dma(s.dma_start(out=xres_t[tsub % 2][:, :],
                                        in_=xres_d[ti128 * 128:(ti128 + 1) * 128, :]))
                LDS_XR = lds.n

                @block.tensor
                def _(t):
                    t.wait_ge(ss.sem, pr_evac[tt4 * 8 + 7])
                    if ti128 >= 2:
                        t.wait_ge(sv.sem, x1_done[ti128 - 2])
                    half = psTr[:, (tsub % 2) * 1024:(tsub % 2) * 1024 + 1024]
                    for cs in range(8):
                        ti = t.transpose(half[:, cs * 128:(cs + 1) * 128],
                                         aprojT[:, cs * 512 + tsub * 128:cs * 512 + tsub * 128 + 128],
                                         identf[:, :])
                    st.inc(ti)
                ST_TR = st.n

                @block.vector
                def _(v):
                    v.wait_ge(st.sem, ST_TR)
                    v.wait_ge(lds.sem, LDS_XR)
                    half = psTr[:, (tsub % 2) * 1024:(tsub % 2) * 1024 + 1024]
                    vi = v.tensor_tensor(scr0[:, :], half[:, 0:1024], xres_t[tsub % 2][:, :], ALU.add)
                    sv.inc(vi)
                    v.wait_ge(sv.sem, sv.n)
                    vi = v.reduce_sum(col0[:, :], scr0[:, :], axis=AX.X)
                    sv.inc(vi)
                    v.wait_ge(sv.sem, sv.n)
                    vi = v.tensor_scalar_mul(col1[:, :], col0[:, :], INV_D)
                    sv.inc(vi)
                    v.wait_ge(sv.sem, sv.n)
                    vi = v.tensor_scalar(scr1[:, :], scr0[:, :], col1[:, 0:1], None, ALU.subtract)
                    sv.inc(vi)
                SV_XC = sv.n

                @block.scalar
                def _(sc):
                    sc.wait_ge(sv.sem, SV_XC)
                    si = sc.activation(scr0[:, :], scr1[:, :], AF.Square,
                                       bias=c_zero[:, 0:1], scale=c_one[:, 0:1],
                                       accum_out=col2[:, :])
                    ss.inc(si)
                    sc.wait_ge(sv.sem, sv.n + 1)
                    si = sc.activation(col3[:, :], col2[:, :], AF.Ln,
                                       bias=c_eps[:, 0:1], scale=c_one[:, 0:1])
                    ss.inc(si)
                    sc.wait_ge(ss.sem, ss.n)
                    si = sc.activation(col3[:, :], col3[:, :], AF.Exp,
                                       bias=c_zero[:, 0:1], scale=c_half[:, 0:1])
                    ss.inc(si)
                SS_SQ = ss.n - 2

                @block.vector
                def _(v):
                    v.wait_ge(ss.sem, SS_SQ)
                    vi = v.tensor_scalar_mul(col2[:, :], col2[:, :], INV_D)
                    sv.inc(vi)
                SS_RSTD = SS_SQ + 2

                @block.vector
                def _(v):
                    v.wait_ge(ss.sem, SS_RSTD)
                    vi = v.tensor_scalar(scr0[:, :], scr1[:, :], col3[:, 0:1], None, ALU.mult)
                    sv.inc(vi)
                    v.wait_ge(sv.sem, sv.n)
                    vi = v.tensor_copy(x1b[:, ti128 * 1024:(ti128 + 1) * 1024], scr0[:, :])
                    sv.inc(vi)
                SV_X1 = sv.n

                @block.sync
                def _(s):
                    s.wait_ge(sv.sem, SV_X1)
                    lds.dma(s.dma_start(out=x1spill[ti128 * 128:(ti128 + 1) * 128, :],
                                        in_=scr0[:, :]))
                x1_done.append(SV_X1)

        if upto <= 3:
            @block.sync
            def _(s):
                s.wait_ge(lds.sem, lds.n)
                s.wait_ge(sv.sem, sv.n)
            return nc
        scope3m.close()
        scope4a = top.enter_context(ExitStack())
        S4a_ = scope4a.enter_context
        psLgT = S4a_(nc.psum_tensor("psLgT", [128, 1024], F32))
        psLg = S4a_(nc.psum_tensor("psLg", [128, 512], F32))
        psCnt = S4a_(nc.psum_tensor("psCnt", [1, 512], F32))
        psPos = S4a_(nc.psum_tensor("psPos", [128, 512], F32))

        g_marks = []
        lg_marks = []
        x1t_marks = []
        x1t_marks_v = []
        PH3_ST = st.n
        PH3_SS = ss.n
        for k in range(NTT):
            @block.sync
            def _(s):
                if k >= 2:
                    s.wait_ge(st.sem, x1t_marks[k - 2])
                lds.dma(s.dma_start(out=xf_t[k % 2][:, :],
                                    in_=x1spill[k * 128:(k + 1) * 128, :]))
            LDS_XF = lds.n

            @block.tensor
            def _(t):
                t.wait_ge(sv.sem, x1_done[k])
                t.wait_ge(lds.sem, LDS_XF)
                if k == 0:
                    t.wait_ge(ss.sem, PH3_SS)
                else:
                    t.wait_ge(sv.sem, x1t_marks_v[k - 1])
                for ccc in range(8):
                    ti = t.transpose(psLgT[:, ccc * 128:(ccc + 1) * 128],
                                     xf_t[k % 2][:, ccc * 128:(ccc + 1) * 128],
                                     identf[:, :])
                st.inc(ti)
                x1t_marks.append(st.n)
            ST_X1T = st.n

            @block.vector
            def _(v):
                v.wait_ge(st.sem, ST_X1T)
                vi = v.tensor_copy(x1Tt[:, :], psLgT[:, 0:1024])
                sv.inc(vi)
                x1t_marks_v.append(sv.n)
            SV_X1T = sv.n

            @block.tensor
            def _(t):
                t.wait_ge(sv.sem, SV_X1T)
                if k >= 1:
                    t.wait_ge(sv.sem, lg_marks[k - 1])
                for ccc in range(8):
                    t.matmul(psLg[:, 0:NE], x1Tt[:, ccc * 128:(ccc + 1) * 128],
                             wgb[:, ccc * NE:(ccc + 1) * NE],
                             start=(ccc == 0), stop=False)
                ti = t.matmul(psLg[:, 0:NE], ones128[0:1, :], bgb[:, :], start=False, stop=True)
                st.inc(ti)
            ST_LG = st.n

            @block.vector
            def _(v):
                v.wait_ge(st.sem, ST_LG)
                vi = v.tensor_copy(Lg[:, :], psLg[:, 0:NE])
                sv.inc(vi)
                lg_marks.append(sv.n)
                v.wait_ge(sv.sem, sv.n)
                vi = v.reduce_max(col0[:, :], Lg[:, :], axis=AX.X)
                sv.inc(vi)
                v.wait_ge(sv.sem, sv.n)
                vi = v.tensor_scalar_mul(col1[:, :], col0[:, :], -1.0)
                sv.inc(vi)
                vi = v.tensor_scalar(ohw[:, :], Lg[:, :], col0[:, 0:1], None, ALU.is_ge)
                sv.inc(vi)
            SV_OH = sv.n

            @block.scalar
            def _(sc):
                sc.wait_ge(sv.sem, SV_OH)
                si = sc.activation(expl[:, :], Lg[:, :], AF.Exp,
                                   bias=col1[:, 0:1], scale=c_one[:, 0:1],
                                   accum_out=col2[:, :])
                ss.inc(si)
            SS_EXPL = ss.n

            @block.vector
            def _(v):
                v.wait_ge(ss.sem, SS_EXPL)
                vi = v.reciprocal(gates[:, k:k + 1], col2[:, :])
                sv.inc(vi)
                vi = v.tensor_copy(ohs[:, k * NE:(k + 1) * NE], ohw[:, :])
                sv.inc(vi)
                v.wait_ge(sv.sem, sv.n)
                vi = v.tensor_tensor(expl[:, :], ohw[:, :], iota16[:, :], ALU.mult)
                sv.inc(vi)
                v.wait_ge(sv.sem, sv.n)
                vi = v.reduce_sum(idxs[:, k:k + 1], expl[:, :], axis=AX.X)
                sv.inc(vi)
                if k >= 1:
                    v.wait_ge(st.sem, g_marks[k - 1])
                    vi = v.tensor_copy(basetile[:, k * NE:(k + 1) * NE], psCnt[0:1, 0:NE])
                    sv.inc(vi)
            SV_BT = sv.n

            @block.tensor
            def _(t):
                t.wait_ge(sv.sem, SV_BT)
                ti = t.matmul(psCnt[0:1, 0:NE], c_one[:, 0:1], ohs[:, k * NE:(k + 1) * NE],
                              start=(k == 0), stop=(k == NTT - 1))
                st.inc(ti)
            g_marks.append(st.n)

        @block.vector
        def _(v):
            v.wait_ge(st.sem, g_marks[-1])
            vi = v.tensor_copy(cntrow[:, :], psCnt[0:1, 0:NE])
            sv.inc(vi)
        SV_CNT = sv.n

        @block.sync
        def _(s):
            s.wait_ge(sv.sem, SV_CNT)
            lds.dma(s.dma_start(out=cc_cnt_in.ap(), in_=cntrow[:, :]))
        LDS_CNT = lds.n

        @block.gpsimd
        def _(g):
            g.wait_ge(lds.sem, LDS_CNT)
            gi = g.collective_compute("AllGather", ALU.bypass, replica_groups=[list(range(8))],
                                      ins=[cc_cnt_in[:, :]], outs=[cc_cnt_out[:, :]])
            cc.inc(gi)
            g.wait_ge(cc.sem, cc.n)
            ldg.dma(g.dma_start(out=cnts_all[:, :], in_=cc_cnt_out.ap()))
        LDG_CNTS = ldg.n

        @block.tensor
        def _(t):
            t.wait_ge(ldg.sem, LDG_CNTS)
            t.wait_ge(sv.sem, SV_CNT)
            ti = t.matmul(psCnt[0:1, 16:32], rankm[:, :], cnts_all[:, :],
                          start=True, stop=True)
            st.inc(ti)
        ST_BC = st.n

        @block.vector
        def _(v):
            v.wait_ge(st.sem, ST_BC)
            vi = v.tensor_copy(basecore[:, :], psCnt[0:1, 16:32])
            sv.inc(vi)
        SV_BC = sv.n

        sc_marks = []
        for k in range(NTT):
            @block.vector
            def _(v):
                v.wait_ge(sv.sem, SV_BC)
                if k >= 1:
                    v.wait_ge(st.sem, sc_marks[k - 1][0])
                vi = v.tensor_tensor(mrow[:, :], basecore[:, :],
                                     basetile[:, k * NE:(k + 1) * NE], ALU.add)
                sv.inc(vi)
            SV_MR = sv.n

            @block.tensor
            def _(t):
                t.wait_ge(sv.sem, SV_MR)
                t.matmul(psPos[:, 0:NE], triu[:, :], ohs[:, k * NE:(k + 1) * NE],
                         start=True, stop=False)
                ti = t.matmul(psPos[:, 0:NE], ones128[0:1, :], mrow[:, :],
                              start=False, stop=True)
                st.inc(ti)
            ST_POS = st.n

            @block.vector
            def _(v):
                v.wait_ge(st.sem, ST_POS)
                vi = v.tensor_tensor(expl[:, :], psPos[:, 0:NE], ohs[:, k * NE:(k + 1) * NE],
                                     ALU.mult)
                sv.inc(vi)
                v.wait_ge(sv.sem, sv.n)
                vi = v.reduce_sum(posf[:, :], expl[:, :], axis=AX.X)
                sv.inc(vi)
                v.wait_ge(sv.sem, sv.n)
                vi = v.tensor_scalar(keepm[:, :], posf[:, :], float(CAP), None, ALU.is_lt)
                sv.inc(vi)
                vi = v.tensor_scalar_mul(slotf[:, :], idxs[:, k:k + 1], float(CAP))
                sv.inc(vi)
                v.wait_ge(sv.sem, sv.n)
                vi = v.tensor_tensor(slotf[:, :], slotf[:, :], posf[:, :], ALU.add)
                sv.inc(vi)
                v.wait_ge(sv.sem, sv.n)
                vi = v.tensor_scalar(col0[:, :], keepm[:, :], -BIG, BIG, ALU.mult, ALU.add)
                sv.inc(vi)
                vi = v.tensor_tensor(slotg[:, :], slotf[:, :], keepm[:, :], ALU.mult)
                sv.inc(vi)
                v.wait_ge(sv.sem, sv.n)
                vi = v.tensor_tensor(slotg[:, :], slotg[:, :], col0[:, :], ALU.add)
                sv.inc(vi)
                v.wait_ge(sv.sem, sv.n)
                vi = v.tensor_copy(slots[:, k:k + 1], slotg[:, :])
                sv.inc(vi)
            SV_SLOT = sv.n

            @block.gpsimd
            def _(g):
                g.wait_ge(sv.sem, SV_SLOT)
                if k == 0:
                    g.wait_ge(lds.sem, LDS_ZERO)
                ldg.dma(g.indirect_dma_start(
                    out=xsort_loc[:, :], out_offset=IOA(slots[:, k:k + 1], 0),
                    in_=x1b[:, k * 1024:(k + 1) * 1024], in_offset=None,
                    bounds_check=NTOK - 1, oob_is_err=False))
            sc_marks.append((st.n, ldg.n))
        LDG_SCAT = ldg.n

        @block.gpsimd
        def _(g):
            g.wait_ge(ldg.sem, LDG_SCAT)
            gi = g.collective_compute("ReduceScatter", ALU.add, replica_groups=[list(range(8))],
                                      ins=[xsort_loc[:, :]], outs=[xsort[:, :]])
            cc.inc(gi)
        CC_RS = cc.n
        if upto <= 4:
            if dbg:
                @block.sync
                def _(s):
                    s.wait_ge(cc.sem, CC_RS)
                    lds.dma(s.dma_start(out=dbg1[:, 0:16], in_=gates[:, :]))
                    lds.dma(s.dma_start(out=dbg2[:, 0:16], in_=idxs[:, :]))
                    lds.dma(s.dma_start(out=dbgxs, in_=xsort.ap()))
                    lds.dma(s.dma_start(out=dbgsl, in_=slots[:, :]))
                    s.wait_ge(lds.sem, lds.n)
            else:
                @block.sync
                def _(s):
                    s.wait_ge(cc.sem, CC_RS)
            return nc

        PH4A_ST = st.n
        PH4A_SS = ss.n
        PH4A_SV = sv.n
        scope4a.close()
        scope3.close()

        scope4b = top.enter_context(ExitStack())
        S4b_ = scope4b.enter_context
        bufT = S4b_(nc.sbuf_tensor("bufT", [128, 8 * 1024], BF16))
        hT = S4b_(nc.sbuf_tensor("hT", [128, 32 * 1024], BF16))
        w2f = S4b_(nc.sbuf_tensor("w2f", [128, 32 * 1024], BF16))
        w1b = [S4b_(nc.sbuf_tensor(f"w1b{i}", [128, 8 * 512], BF16)) for i in range(2)]
        bufS = [S4b_(nc.sbuf_tensor(f"bufS{i}", [128, 1024], BF16)) for i in range(2)]
        b1S = S4b_(nc.sbuf_tensor("b1S", [32, 128], F32))
        b1T = S4b_(nc.sbuf_tensor("b1T", [128, 32], F32))
        b2S = S4b_(nc.sbuf_tensor("b2S", [8, 128], F32))
        b2T = S4b_(nc.sbuf_tensor("b2T", [128, 8], F32))
        ytmp = S4b_(nc.sbuf_tensor("ytmp", [128, 512], F32))
        ytok = S4b_(nc.sbuf_tensor("ytok", [128, 4 * 1024], BF16))

        lds_y = []
        prev_h_last = 0
        prev_y_last = 0
        for e in range(EPC):
            scope_p = top.enter_context(ExitStack())
            psBufT = scope_p.enter_context(nc.psum_tensor(f"psBufT{e}", [128, 1024], BF16))
            psTb = scope_p.enter_context(nc.psum_tensor(f"psTb{e}", [128, 512], F32))

            @block.sync
            def _(s):
                s.wait_ge(ldg.sem, LDG_SCAT)
                lds.dma(s.dma_start(out=b1S[:, :], in_=b1_d[e]))
                lds.dma(s.dma_start(out=b2S[:, :], in_=b2_d[e]))
            LDS_B = lds.n

            @block.tensor
            def _(t):
                t.wait_ge(lds.sem, LDS_B)
                if e == 0:
                    t.wait_ge(ss.sem, PH4A_SS)
                    t.wait_ge(sv.sem, PH4A_SV)
                else:
                    t.wait_ge(ss.sem, ss.n)
                    t.wait_ge(sv.sem, sv.n)
                t.transpose(psTb[:, 0:32], b1S[:, :], identf[0:32, 0:32])
                ti = t.transpose(psTb[:, 32:40], b2S[:, :], identf[0:8, 0:8])
                st.inc(ti)
            ST_B = st.n

            @block.vector
            def _(v):
                v.wait_ge(st.sem, ST_B)
                v.tensor_copy(b1T[:, :], psTb[:, 0:32])
                vi = v.tensor_copy(b2T[:, :], psTb[:, 32:40])
                sv.inc(vi)
            SV_B = sv.n

            buf_marks = []
            for i in range(8):
                @block.sync
                def _(s):
                    s.wait_ge(cc.sem, CC_RS)
                    if i >= 2:
                        s.wait_ge(sv.sem, buf_marks[i - 2])
                    lds.dma(s.dma_start(
                        out=bufS[i % 2][:, :],
                        in_=xsort[(e * CAP + i * 128):(e * CAP + (i + 1) * 128), :]))
                LDS_BUF = lds.n

                @block.tensor
                def _(t):
                    t.wait_ge(lds.sem, LDS_BUF)
                    if i == 0:
                        t.wait_ge(sv.sem, SV_B)
                    if i >= 1:
                        t.wait_ge(sv.sem, buf_marks[i - 1])
                    for ccc in range(8):
                        ti = t.transpose(psBufT[:, ccc * 128:(ccc + 1) * 128],
                                         bufS[i % 2][:, ccc * 128:(ccc + 1) * 128],
                                         identb[:, :])
                    st.inc(ti)
                ST_BUF = st.n

                @block.vector
                def _(v):
                    v.wait_ge(st.sem, ST_BUF)
                    for ccc in range(8):
                        vi = v.tensor_copy(bufT[:, ccc * 1024 + i * 128:ccc * 1024 + (i + 1) * 128],
                                           psBufT[:, ccc * 128:(ccc + 1) * 128])
                    sv.inc(vi)
                    buf_marks.append(sv.n)
            SV_BUFT = sv.n

            if dbg and upto == 5 and e == 0 and not _NODUMP:
                @block.gpsimd
                def _(g):
                    g.wait_ge(sv.sem, SV_BUFT)
                    ldg.dma(g.dma_start(out=dbgbt[:, :], in_=bufT[:, 0:2048]))
            scope_p.close()
            scope_h = top.enter_context(ExitStack())
            psH = scope_h.enter_context(nc.psum_tensor(f"psH{e}", [128, 2048], F32))

            h_mm = []
            ss_base_h = ss.n
            for hg in range(8):
                @block.gpsimd
                def _(g):
                    if hg == 0:
                        if e == 0:
                            g.wait_ge(ldg.sem, LDG_SCAT)
                        else:
                            g.wait_ge(st.sem, prev_h_last)
                    if hg >= 2:
                        g.wait_ge(st.sem, h_mm[-1])
                    ldg.dma(g.dma_start(
                        out=w1b[hg % 2][:, :].rearrange("p (cc h) -> p cc h", cc=8),
                        in_=w1_d[e, :, hg * 512:(hg + 1) * 512].rearrange("(cc p) h -> p cc h", p=128)))
                LDG_W1 = ldg.n

                @block.gpsimd
                def _(g):
                    if e > 0 and hg == 0:
                        g.wait_ge(st.sem, prev_y_last)
                    ldg.dma(g.dma_start(
                        out=w2f[:, hg * 4096:(hg + 1) * 4096].rearrange("p (q c) -> p q c", q=4),
                        in_=w2_d[e, hg * 512:(hg + 1) * 512, :].rearrange("(q p) c -> p q c", p=128)))
                LDG_W2F = ldg.n

                for hs4 in range(4):
                    for sti in range(2):
                        bsel = (hs4 * 2 + sti) % 4

                        @block.tensor
                        def _(t):
                            t.wait_ge(ldg.sem, LDG_W1)
                            if hg == 0 and hs4 == 0 and sti == 0:
                                t.wait_ge(sv.sem, SV_BUFT)
                            gi = hg * 8 + hs4 * 2 + sti
                            if gi >= 4:
                                t.wait_ge(ss.sem, ss_base_h + gi - 3)
                            bank = psH[:, bsel * 512:(bsel + 1) * 512]
                            for ccc in range(8):
                                ti = t.matmul(bank,
                                              w1b[hg % 2][:, ccc * 512 + hs4 * 128:ccc * 512 + hs4 * 128 + 128],
                                              bufT[:, ccc * 1024 + sti * 512:ccc * 1024 + sti * 512 + 512],
                                              start=(ccc == 0), stop=(ccc == 7))
                            st.inc(ti)
                            h_mm.append(st.n)

            @block.scalar
            def _(sc):
                for gi in range(64):
                    hg, hs4, sti = gi // 8, (gi % 8) // 2, gi % 2
                    bsel = (hs4 * 2 + sti) % 4
                    sc.wait_ge(st.sem, h_mm[gi])
                    bank = psH[:, bsel * 512:(bsel + 1) * 512]
                    hs = hg * 4 + hs4
                    si = sc.activation(hT[:, hs * 1024 + sti * 512:hs * 1024 + sti * 512 + 512],
                                       bank, AF.Relu,
                                       bias=b1T[:, hs:hs + 1], scale=c_one[:, 0:1])
                    ss.inc(si)
            SS_H = ss.n
            prev_h_last = h_mm[-1]

            if dbg and upto == 5 and e == 0 and not _NODUMP:
                @block.gpsimd
                def _(g):
                    g.wait_ge(ss.sem, SS_H)
                    ldg.dma(g.dma_start(out=dbgh[:, 0:2048], in_=hT[:, 0:2048]))
                    ldg.dma(g.dma_start(out=dbgh[:, 2048:4096], in_=hT[:, 16 * 1024:16 * 1024 + 2048]))
            scope_h.close()
            scope_y = top.enter_context(ExitStack())
            psY = scope_y.enter_context(nc.psum_tensor(f"psY{e}", [128, 4096], F32))

            for sti in range(2):
                y_mm = []

                @block.tensor
                def _(t):
                    t.wait_ge(ldg.sem, LDG_W2F)
                    if sti == 0:
                        t.wait_ge(ss.sem, SS_H)
                    if sti == 1:
                        t.wait_ge(sv.sem, sv.n)
                    for cs in range(8):
                        bank = psY[:, cs * 512:(cs + 1) * 512]
                        for hc in range(32):
                            ti = t.matmul(bank,
                                          w2f[:, hc * 1024 + cs * 128:hc * 1024 + cs * 128 + 128],
                                          hT[:, hc * 1024 + sti * 512:hc * 1024 + sti * 512 + 512],
                                          start=(hc == 0), stop=(hc == 31))
                    st.inc(ti)
                    y_mm.append(st.n)
                Y_MM = y_mm[-1]
                ss_base_y = ss.n
                st_base_y = st.n

                @block.scalar
                def _(sc):
                    for cs in range(8):
                        sc.wait_ge(st.sem, Y_MM)
                        if cs >= 1:
                            sc.wait_ge(st.sem, st_base_y + cs)
                        bank = psY[:, cs * 512:(cs + 1) * 512]
                        si = sc.activation(ytmp[:, :], bank, AF.Identity,
                                           bias=b2T[:, cs:cs + 1], scale=c_one[:, 0:1])
                        ss.inc(si)

                for cs in range(8):
                    @block.tensor
                    def _(t):
                        t.wait_ge(ss.sem, ss_base_y + cs + 1)
                        for tb in range(4):
                            ti = t.transpose(psY[:, cs * 512 + tb * 128:cs * 512 + tb * 128 + 128],
                                             ytmp[:, tb * 128:(tb + 1) * 128],
                                             identf[:, :])
                        st.inc(ti)
                    ST_YT = st.n

                    @block.vector
                    def _(v):
                        v.wait_ge(st.sem, ST_YT)
                        if cs == 0 and lds_y:
                            v.wait_ge(lds.sem, lds_y[-1])
                        for tb in range(4):
                            vi = v.tensor_copy(ytok[:, tb * 1024 + cs * 128:tb * 1024 + cs * 128 + 128],
                                               psY[:, cs * 512 + tb * 128:cs * 512 + tb * 128 + 128])
                        sv.inc(vi)
                SV_YTOK = sv.n

                @block.sync
                def _(s):
                    s.wait_ge(sv.sem, SV_YTOK)
                    for tb in range(4):
                        r0 = e * CAP + sti * 512 + tb * 128
                        lds.dma(s.dma_start(out=cc_y_in[r0:r0 + 128, :],
                                            in_=ytok[:, tb * 1024:(tb + 1) * 1024]))
                lds_y.append(lds.n)

            prev_y_last = st.n
            scope_y.close()

        @block.gpsimd
        def _(g):
            g.wait_ge(lds.sem, lds_y[-1])
            gi = g.collective_compute("AllGather", ALU.bypass, replica_groups=[list(range(8))],
                                      ins=[cc_y_in[:, :]], outs=[cc_y_out[:, :]])
            cc.inc(gi)
        CC_Y = cc.n
        if upto <= 5:
            @block.sync
            def _(s):
                s.wait_ge(cc.sem, CC_Y)
                if dbg:
                    lds.dma(s.dma_start(out=dbgy, in_=cc_y_in.ap()))
                    lds.dma(s.dma_start(out=dbgxs5, in_=xsort.ap()))
                    s.wait_ge(lds.sem, lds.n)
            if dbg:
                @block.gpsimd
                def _(g):
                    g.wait_ge(cc.sem, CC_Y)
                    g.wait_ge(ldg.sem, ldg.n)
            return nc

        PH4B_SS = ss.n
        PH4B_SV = sv.n
        scope4b.close()

        scope4c = top.enter_context(ExitStack())
        S4c_ = scope4c.enter_context
        ygath = [S4c_(nc.sbuf_tensor(f"ygath{i}", [128, 1024], BF16)) for i in range(2)]
        x1t = [S4c_(nc.sbuf_tensor(f"x1t{i}", [128, 1024], F32)) for i in range(2)]
        outt = [S4c_(nc.sbuf_tensor(f"outt{i}", [128, 1024], F32)) for i in range(2)]

        yz_marks = []
        so_marks = []
        for k in range(NTT):
            @block.vector
            def _(v):
                if k == 0:
                    v.wait_ge(ss.sem, PH4B_SS)
                    v.wait_ge(lds.sem, lds_y[-1])
                if k >= 2:
                    v.wait_ge(ldg.sem, yz_marks[k - 2][1])
                vi = v.memset(ygath[k % 2][:, :], 0.0)
                sv.inc(vi)
            SV_YZ = sv.n

            @block.gpsimd
            def _(g):
                g.wait_ge(cc.sem, CC_Y)
                g.wait_ge(sv.sem, SV_YZ)
                ldg.dma(g.indirect_dma_start(
                    out=ygath[k % 2][:, :], out_offset=None,
                    in_=cc_y_out[:, :], in_offset=IOA(slots[:, k:k + 1], 0),
                    bounds_check=NTOK - 1, oob_is_err=False))
            LDG_YG = ldg.n
            yz_marks.append((SV_YZ, LDG_YG))

            @block.sync
            def _(s):
                if k >= 2:
                    s.wait_ge(sv.sem, so_marks[k - 2][0])
                lds.dma(s.dma_start(out=x1t[k % 2][:, :],
                                    in_=x1spill[k * 128:(k + 1) * 128, :]))
            LDS_X1T = lds.n

            @block.vector
            def _(v):
                v.wait_ge(ldg.sem, LDG_YG)
                v.wait_ge(lds.sem, LDS_X1T)
                vi = v.tensor_scalar(scr1[:, :], ygath[k % 2][:, :], gates[:, k:k + 1],
                                     None, ALU.mult)
                sv.inc(vi)
                v.wait_ge(sv.sem, sv.n)
                vi = v.tensor_tensor(scr0[:, :], x1t[k % 2][:, :], scr1[:, :], ALU.add)
                sv.inc(vi)
                v.wait_ge(sv.sem, sv.n)
                vi = v.reduce_sum(col0[:, :], scr0[:, :], axis=AX.X)
                sv.inc(vi)
                v.wait_ge(sv.sem, sv.n)
                vi = v.tensor_scalar_mul(col1[:, :], col0[:, :], INV_D)
                sv.inc(vi)
                v.wait_ge(sv.sem, sv.n)
                vi = v.tensor_scalar(scr1[:, :], scr0[:, :], col1[:, 0:1], None, ALU.subtract)
                sv.inc(vi)
            SV_XC2 = sv.n

            @block.scalar
            def _(sc):
                sc.wait_ge(sv.sem, SV_XC2)
                si = sc.activation(scr0[:, :], scr1[:, :], AF.Square,
                                   bias=c_zero[:, 0:1], scale=c_one[:, 0:1],
                                   accum_out=col2[:, :])
                ss.inc(si)
                sc.wait_ge(sv.sem, sv.n + 1)
                si = sc.activation(col3[:, :], col2[:, :], AF.Ln,
                                   bias=c_eps[:, 0:1], scale=c_one[:, 0:1])
                ss.inc(si)
                sc.wait_ge(ss.sem, ss.n)
                si = sc.activation(col3[:, :], col3[:, :], AF.Exp,
                                   bias=c_zero[:, 0:1], scale=c_half[:, 0:1])
                ss.inc(si)
            SS_SQ2 = ss.n - 2

            @block.vector
            def _(v):
                v.wait_ge(ss.sem, SS_SQ2)
                vi = v.tensor_scalar_mul(col2[:, :], col2[:, :], INV_D)
                sv.inc(vi)
            SS_RSTD2 = SS_SQ2 + 2

            @block.vector
            def _(v):
                v.wait_ge(ss.sem, SS_RSTD2)
                if k >= 2:
                    v.wait_ge(so.sem, so_marks[k - 2][1])
                vi = v.tensor_scalar(outt[k % 2][:, :], scr1[:, :], col3[:, 0:1], None, ALU.mult)
                sv.inc(vi)
            SV_OUT = sv.n

            @block.sync
            def _(s):
                s.wait_ge(sv.sem, SV_OUT)
                so.dma(s.dma_start(out=out_d[k * 128:(k + 1) * 128, :], in_=outt[k % 2][:, :]))
            so_marks.append((SV_OUT, so.n))

        @block.sync
        def _(s):
            s.wait_ge(so.sem, so_marks[-1][1])

        scope4c.close()

    return nc


def kernel(L0_attn_x, L0_attn_w_qkv, L0_attn_w_proj, L0_attn_b_proj,
           L0_ffn_Wg, L0_ffn_bg, L0_ffn_W1, L0_ffn_b1, L0_ffn_W2, L0_ffn_b2):
    x = np.asarray(L0_attn_x, dtype=np.float32)
    w_qkv = np.asarray(L0_attn_w_qkv, dtype=np.float32)
    w_proj = np.ascontiguousarray(np.asarray(L0_attn_w_proj, dtype=np.float32))
    b_proj = np.asarray(L0_attn_b_proj, dtype=np.float32)
    Wg = np.ascontiguousarray(np.asarray(L0_ffn_Wg, dtype=np.float32))
    bg = np.asarray(L0_ffn_bg, dtype=np.float32)
    W1 = np.asarray(L0_ffn_W1, dtype=np.float32)
    b1 = np.asarray(L0_ffn_b1, dtype=np.float32)
    W2 = np.asarray(L0_ffn_W2, dtype=np.float32)
    b2 = np.asarray(L0_ffn_b2, dtype=np.float32)

    xt = [np.ascontiguousarray(x[b].T) for b in range(B)]
    in_maps = []
    for c in range(8):
        b = c // 4
        g = c % 4
        h0 = 4 * g
        wq = w_qkv[:, 64 * h0: 64 * h0 + 256]
        wk = w_qkv[:, D + 64 * h0: D + 64 * h0 + 256]
        wv = w_qkv[:, 2 * D + 64 * h0: 2 * D + 64 * h0 + 256]
        wsel = np.ascontiguousarray(
            np.concatenate([wq[:, 0:128], wq[:, 128:256], wk[:, 0:128], wk[:, 128:256], wv], axis=1))
        toff = np.full((128, 32), g * TOK, dtype=np.int32)
        in_maps.append({
            "xt": xt[b],
            "xres": np.ascontiguousarray(x[b, g * TOK:(g + 1) * TOK, :]),
            "wqkv_sel": wsel,
            "wproj": w_proj,
            "bprojr": np.ascontiguousarray(b_proj.reshape(8, 128)),
            "wg": Wg,
            "bg1": np.ascontiguousarray(bg.reshape(1, NE)),
            "w1": np.ascontiguousarray(W1[2 * c:2 * c + 2]),
            "b1r": np.ascontiguousarray(b1[2 * c:2 * c + 2].reshape(EPC, 32, 128)),
            "w2": np.ascontiguousarray(W2[2 * c:2 * c + 2]),
            "b2r": np.ascontiguousarray(b2[2 * c:2 * c + 2].reshape(EPC, 8, 128)),
            "rankmask": (np.arange(8) < c).astype(np.float32).reshape(8, 1),
            "toff": toff,
        })

    nc = build_program()
    res = run_bass_kernel_spmd(nc, in_maps, core_ids=list(range(8)))
    parts = [res.results[c]["out"] for c in range(8)]
    full = np.concatenate(parts, axis=0).reshape(B, N, D)
    return full.astype(np.float32)


# version 14 (final)
# speedup vs baseline: 1.0167x; 1.0038x over previous
import numpy as np
import concourse.bass as bass
import concourse.mybir as mybir
from concourse.bass_utils import run_bass_kernel_spmd
from contextlib import ExitStack

F32 = mybir.dt.float32
BF16 = mybir.dt.bfloat16
I32 = mybir.dt.int32
AF = mybir.ActivationFunctionType
ALU = mybir.AluOpType
AX = mybir.AxisListType
IOA = bass.IndirectOffsetOnAxis

B = 2
N = 8192
D = 1024
HD = 64
TOK = 2048
NTOK = B * N
NE = 16
CAP = 1024
HID = 4096
EPC = 2
NQT = N // 512
NKB = N // 128
SCALE = 0.125
CSHIFT = 16.0
EPS = 1e-5
BIG = 1.0e9
NTT = TOK // 128
INV_D = 1.0 / D


class Ctr:
    def __init__(self, sem):
        self.sem = sem
        self.n = 0

    def inc(self, inst, k=1):
        inst.then_inc(self.sem, k)
        self.n += k
        return self.n

    def dma(self, inst):
        inst.then_inc(self.sem, 16)
        self.n += 16
        return self.n


_LDW_PATCHED = False


def _enable_ldw_opt():
    global _LDW_PATCHED
    if _LDW_PATCHED:
        return
    import concourse.bass_utils as _bu
    _real = _bu.run_command

    def _patched(cmd, **kw):
        cmd = [c.replace("--enable-ldw-opt=false", "--enable-ldw-opt=true")
               if isinstance(c, str) else c for c in cmd]
        return _real(cmd, **kw)

    _bu.run_command = _patched
    _LDW_PATCHED = True


def build_program(upto=99, dbg=False, _NODUMP=False):
    _enable_ldw_opt()
    nc = bass.Bass("TRN2", target_bir_lowering=False, debug=False)

    xt_d = nc.dram_tensor("xt", [D, N], F32, kind="ExternalInput").ap()
    xres_d = nc.dram_tensor("xres", [TOK, D], F32, kind="ExternalInput").ap()
    wqkv_d = nc.dram_tensor("wqkv_sel", [D, 768], F32, kind="ExternalInput").ap()
    wproj_d = nc.dram_tensor("wproj", [D, D], F32, kind="ExternalInput").ap()
    bproj_d = nc.dram_tensor("bprojr", [8, 128], F32, kind="ExternalInput").ap()
    wg_d = nc.dram_tensor("wg", [D, NE], F32, kind="ExternalInput").ap()
    bg_d = nc.dram_tensor("bg1", [1, NE], F32, kind="ExternalInput").ap()
    w1_d = nc.dram_tensor("w1", [EPC, D, HID], F32, kind="ExternalInput").ap()
    b1_d = nc.dram_tensor("b1r", [EPC, 32, 128], F32, kind="ExternalInput").ap()
    w2_d = nc.dram_tensor("w2", [EPC, HID, D], F32, kind="ExternalInput").ap()
    b2_d = nc.dram_tensor("b2r", [EPC, 8, 128], F32, kind="ExternalInput").ap()
    rankmask_d = nc.dram_tensor("rankmask", [8, 1], F32, kind="ExternalInput").ap()
    toff_d = nc.dram_tensor("toff", [128, 32], I32, kind="ExternalInput").ap()
    out_d = nc.dram_tensor("out", [TOK, D], F32, kind="ExternalOutput").ap()

    if dbg:
        dbg1 = nc.dram_tensor("dbg1", [128, 2048], F32, kind="ExternalOutput").ap()
        dbg2 = nc.dram_tensor("dbg2", [128, 2048], F32, kind="ExternalOutput").ap()
    cc_att_in = nc.dram_tensor("cc_att_in", [256, N], BF16)
    cc_att_out = nc.dram_tensor("cc_att_out", [D, N], BF16)
    x1spill = nc.dram_tensor("x1spill", [TOK, D], F32,
                              kind="ExternalOutput" if (dbg and upto == 3) else "Internal")
    xsort_loc = nc.dram_tensor("xsort_loc", [NTOK, D], BF16)
    xsort = nc.dram_tensor("xsort", [TOK, D], BF16)
    if dbg and upto == 4:
        dbgxs = nc.dram_tensor("dbgxs", [TOK, D], BF16, kind="ExternalOutput").ap()
        dbgsl = nc.dram_tensor("dbgsl", [128, NTT], I32, kind="ExternalOutput").ap()
    cc_cnt_in = nc.dram_tensor("cc_cnt_in", [1, NE], F32)
    cc_cnt_out = nc.dram_tensor("cc_cnt_out", [8, NE], F32, addr_space="Shared")
    cc_y_in = nc.dram_tensor("cc_y_in", [TOK, D], BF16)
    if dbg and upto == 5:
        dbgy = nc.dram_tensor("dbgy", [TOK, D], F32, kind="ExternalOutput").ap()
        dbgxs5 = nc.dram_tensor("dbgxs5", [TOK, D], BF16, kind="ExternalOutput").ap()
        dbgh = nc.dram_tensor("dbgh", [128, 4096], BF16, kind="ExternalOutput").ap()
        dbgbt = nc.dram_tensor("dbgbt", [128, 2048], BF16, kind="ExternalOutput").ap()
    cc_y_out = nc.dram_tensor("cc_y_out", [NTOK, D], BF16, addr_space="Shared")

    top = ExitStack()
    with top:
        T_ = top.enter_context

        ones128 = T_(nc.sbuf_tensor("ones128", [128, 128], F32))
        identb = T_(nc.sbuf_tensor("identb", [128, 128], BF16))
        identf = T_(nc.sbuf_tensor("identf", [128, 128], F32))
        triu = T_(nc.sbuf_tensor("triu", [128, 128], F32))
        iota16 = T_(nc.sbuf_tensor("iota16", [128, NE], F32))
        onescb = T_(nc.sbuf_tensor("onescb", [128, 1], BF16))
        onesrb = T_(nc.sbuf_tensor("onesrb", [1, 128], BF16))
        c_one = T_(nc.sbuf_tensor("c_one", [128, 1], F32))
        c_zero = T_(nc.sbuf_tensor("c_zero", [128, 1], F32))
        c_exps = T_(nc.sbuf_tensor("c_exps", [128, 1], F32))
        c_expb = T_(nc.sbuf_tensor("c_expb", [128, 1], F32))
        c_half = T_(nc.sbuf_tensor("c_half", [128, 1], F32))
        c_invd = T_(nc.sbuf_tensor("c_invd", [128, 1], F32))
        c_eps = T_(nc.sbuf_tensor("c_eps", [128, 1], F32))
        c_big = T_(nc.sbuf_tensor("c_big", [128, 1], F32))
        zerot = T_(nc.sbuf_tensor("zerot", [128, 1024], BF16))
        scr0 = T_(nc.sbuf_tensor("scr0", [128, 1024], F32))
        scr1 = T_(nc.sbuf_tensor("scr1", [128, 1024], F32))
        col0 = T_(nc.sbuf_tensor("col0", [128, 1], F32))
        col1 = T_(nc.sbuf_tensor("col1", [128, 1], F32))
        col2 = T_(nc.sbuf_tensor("col2", [128, 1], F32))
        col3 = T_(nc.sbuf_tensor("col3", [128, 1], F32))
        Lg = T_(nc.sbuf_tensor("Lg", [128, NE], F32))
        expl = T_(nc.sbuf_tensor("expl", [128, NE], F32))
        ohw = T_(nc.sbuf_tensor("ohw", [128, NE], F32))
        mrow = T_(nc.sbuf_tensor("mrow", [1, NE], F32))
        ohs = T_(nc.sbuf_tensor("ohs", [128, NTT * NE], F32))
        idxs = T_(nc.sbuf_tensor("idxs", [128, NTT], F32))
        gates = T_(nc.sbuf_tensor("gates", [128, NTT], F32))
        slots = T_(nc.sbuf_tensor("slots", [128, NTT], I32))
        posf = T_(nc.sbuf_tensor("posf", [128, 1], F32))
        keepm = T_(nc.sbuf_tensor("keepm", [128, 1], F32))
        slotf = T_(nc.sbuf_tensor("slotf", [128, 1], F32))
        slotg = T_(nc.sbuf_tensor("slotg", [128, 1], F32))
        basetile = T_(nc.sbuf_tensor("basetile", [1, NTT * NE], F32))
        basecore = T_(nc.sbuf_tensor("basecore", [1, NE], F32))
        cnts_all = T_(nc.sbuf_tensor("cnts_all", [8, NE], F32))
        cntrow = T_(nc.sbuf_tensor("cntrow", [1, NE], F32))
        rankm = T_(nc.sbuf_tensor("rankm", [8, 1], F32))
        bgb = T_(nc.sbuf_tensor("bgb", [1, NE], F32))
        wgb = T_(nc.sbuf_tensor("wgb", [128, 8 * NE], F32))
        bprojT = T_(nc.sbuf_tensor("bprojT", [128, 8], F32))
        bprojS = T_(nc.sbuf_tensor("bprojS", [8, 128], F32))

        lds = Ctr(T_(nc.semaphore("lds")))
        ldg = Ctr(T_(nc.semaphore("ldg")))
        st = Ctr(T_(nc.semaphore("st")))
        ss = Ctr(T_(nc.semaphore("ss")))
        sv = Ctr(T_(nc.semaphore("sv")))
        sg = Ctr(T_(nc.semaphore("sg")))
        cc = Ctr(T_(nc.semaphore("cc")))
        sqk = Ctr(T_(nc.semaphore("sqk")))
        sxp = Ctr(T_(nc.semaphore("sxp")))
        so = Ctr(T_(nc.semaphore("so")))

        block = T_(nc.Block())

        @block.vector
        def _(v):
            v.memset(ones128[:, :], 1.0)
            v.memset(onescb[:, :], 1.0)
            v.memset(onesrb[:, :], 1.0)
            v.memset(c_one[:, :], 1.0)
            v.memset(c_zero[:, :], 0.0)
            v.memset(c_exps[:, :], SCALE)
            v.memset(c_expb[:, :], -CSHIFT)
            v.memset(c_half[:, :], -0.5)
            v.memset(c_invd[:, :], INV_D)
            v.memset(c_eps[:, :], EPS)
            v.memset(c_big[:, :], BIG)
            v.memset(zerot[:, :], 0.0)
            vi = v.memset(basetile[:, 0:NE], 0.0)
            sv.inc(vi)
        SV_CONST = sv.n

        @block.gpsimd
        def _(g):
            g.wait_ge(sv.sem, SV_CONST)
            g.iota(iota16[:, :], pattern=[[1, NE]], base=0, channel_multiplier=0,
                   allow_small_or_imprecise_dtypes=True)
            g.affine_select(identb[:, :], ones128[:, :], pattern=[[-1, 128]],
                            compare_op=ALU.is_equal, fill=0.0, base=0, channel_multiplier=1)
            g.affine_select(identf[:, :], ones128[:, :], pattern=[[-1, 128]],
                            compare_op=ALU.is_equal, fill=0.0, base=0, channel_multiplier=1)
            gi = g.affine_select(triu[:, :], ones128[:, :], pattern=[[1, 128]],
                                 compare_op=ALU.is_gt, fill=0.0, base=0, channel_multiplier=-1)
            sg.inc(gi)
        SG_CONST = sg.n

        @block.sync
        def _(s):
            lds.dma(s.dma_start(out=wgb[:, :].rearrange("p (cc e) -> p cc e", cc=8),
                                in_=wg_d.rearrange("(cc p) e -> p cc e", p=128)))
            lds.dma(s.dma_start(out=bgb[:, :], in_=bg_d))

        @block.sync
        def _(s):
            lds.dma(s.dma_start(out=bprojS[:, :], in_=bproj_d))
            lds.dma(s.dma_start(out=rankm[:, :], in_=rankmask_d))
        LDS_W0 = lds.n

        @block.sync
        def _(s):
            s.wait_ge(sv.sem, SV_CONST)
            for i in range(NTOK // 128):
                lds.dma(s.dma_start(out=xsort_loc[128 * i:128 * (i + 1), :], in_=zerot[:, :]))
        LDS_ZERO = lds.n

        scopeA = top.enter_context(ExitStack())
        A_ = scopeA.enter_context
        kTp = [A_(nc.sbuf_tensor(f"kTp{p}", [128, N], BF16)) for p in range(2)]
        qTp = [A_(nc.sbuf_tensor(f"qTp{p}", [128, N], BF16)) for p in range(2)]
        vh = A_(nc.sbuf_tensor("vh", [128, NKB * 4 * 65], BF16))
        rowt = A_(nc.sbuf_tensor("rowt", [128, 1024], F32))
        obuf2 = A_(nc.sbuf_tensor("obuf2", [64, 512], BF16))
        ocopyA = A_(nc.sbuf_tensor("ocopyA", [128, 512], F32))
        ocopyB = A_(nc.sbuf_tensor("ocopyB", [128, 512], F32))
        erow = A_(nc.sbuf_tensor("erow", [128, 512], F32))
        erow2 = A_(nc.sbuf_tensor("erow2", [128, 512], F32))

        scope1 = top.enter_context(ExitStack())
        S1_ = scope1.enter_context
        wqkvb = S1_(nc.sbuf_tensor("wqkvb", [128, 8 * 768], BF16))
        xtq = [S1_(nc.sbuf_tensor(f"xtq{i}", [128, 8 * 512], BF16)) for i in range(2)]
        psQ = S1_(nc.psum_tensor("psQ", [128, 2048], F32))
        psV = S1_(nc.psum_tensor("psV", [128, 1024], F32))

        @block.gpsimd
        def _(g):
            ldg.dma(g.dma_start(out=wqkvb[:, :].rearrange("p (cc o) -> p cc o", cc=8),
                                in_=wqkv_d.rearrange("(cc p) o -> p cc o", p=128)))
        LDG_WQKV = ldg.n

        @block.vector
        def _(v):
            vi = v.memset(vh[:, :], 1.0)
            sv.inc(vi)

        ldg_x = []
        qk_done = []
        v_done = []
        ev_qk = []
        ev_v = []
        for tt in range(16):
            @block.gpsimd
            def _(g):
                if tt >= 2:
                    g.wait_ge(st.sem, v_done[tt - 2])
                ldg.dma(g.dma_start(
                    out=xtq[tt % 2][:, :].rearrange("p (cc t) -> p cc t", cc=8),
                    in_=xt_d[:, 512 * tt:512 * (tt + 1)].rearrange("(cc p) t -> p cc t", p=128)))
            ldg_x.append(ldg.n)

            @block.tensor
            def _(t):
                t.wait_ge(ldg.sem, ldg_x[tt])
                if tt == 0:
                    t.wait_ge(ldg.sem, LDG_WQKV)
                    t.wait_ge(sg.sem, SG_CONST)
                    t.wait_ge(sv.sem, SV_CONST)
                if tt >= 1:
                    t.wait_ge(ss.sem, ev_qk[tt - 1])
                for ccc in range(8):
                    xs = xtq[tt % 2][:, ccc * 512:(ccc + 1) * 512]
                    fl = dict(start=(ccc == 0), stop=(ccc == 7))
                    t.matmul(psQ[:, 0:512], wqkvb[:, ccc * 768 + 0:ccc * 768 + 128], xs, **fl)
                    t.matmul(psQ[:, 512:1024], wqkvb[:, ccc * 768 + 128:ccc * 768 + 256], xs, **fl)
                    t.matmul(psQ[:, 1024:1536], wqkvb[:, ccc * 768 + 256:ccc * 768 + 384], xs, **fl)
                    ti = t.matmul(psQ[:, 1536:2048], wqkvb[:, ccc * 768 + 384:ccc * 768 + 512], xs, **fl)
                st.inc(ti)
                qk_done.append(st.n)
                if tt >= 1:
                    t.wait_ge(sv.sem, ev_v[tt - 1])
                for vsub in range(4):
                    vps = psV[:, vsub * 256:(vsub + 1) * 256]
                    for ccc in range(8):
                        ti = t.matmul(vps,
                                      xtq[tt % 2][:, ccc * 512 + vsub * 128:ccc * 512 + vsub * 128 + 128],
                                      wqkvb[:, ccc * 768 + 512:ccc * 768 + 768],
                                      start=(ccc == 0), stop=(ccc == 7))
                st.inc(ti)
                v_done.append(st.n)

            @block.scalar
            def _(sc):
                sc.wait_ge(st.sem, qk_done[tt])
                qc = slice(tt * 512, tt * 512 + 512)
                sc.activation(qTp[0][:, qc], psQ[:, 0:512], AF.Identity,
                              bias=c_zero[:, 0:1], scale=c_one[:, 0:1])
                sc.activation(qTp[1][:, qc], psQ[:, 512:1024], AF.Identity,
                              bias=c_zero[:, 0:1], scale=c_one[:, 0:1])
                sc.activation(kTp[0][:, qc], psQ[:, 1024:1536], AF.Identity,
                              bias=c_zero[:, 0:1], scale=c_one[:, 0:1])
                si = sc.activation(kTp[1][:, qc], psQ[:, 1536:2048], AF.Identity,
                                   bias=c_zero[:, 0:1], scale=c_one[:, 0:1])
                ss.inc(si)
                ev_qk.append(ss.n)

            @block.vector
            def _(v):
                v.wait_ge(st.sem, v_done[tt])
                for vsub in range(4):
                    kb = tt * 4 + vsub
                    vps = psV[:, vsub * 256:(vsub + 1) * 256]
                    for h in range(4):
                        vi = v.tensor_copy(vh[:, (kb * 4 + h) * 65:(kb * 4 + h) * 65 + HD],
                                           vps[:, h * HD:(h + 1) * HD])
                sv.inc(vi)
                ev_v.append(sv.n)

        if upto <= 1:
            if dbg:
                @block.gpsimd
                def _(g):
                    g.wait_ge(ss.sem, ev_qk[-1])
                    g.wait_ge(sv.sem, ev_v[-1])
                    ldg.dma(g.dma_start(out=dbg1[:, 0:1024], in_=qTp[0][:, 0:1024]))
                    ldg.dma(g.dma_start(out=dbg1[:, 1024:2048], in_=kTp[1][:, 0:1024]))
                    ldg.dma(g.dma_start(out=dbg2[:, 0:1024], in_=vh[:, 0:1024]))
                    ldg.dma(g.dma_start(out=dbg2[:, 1024:2048], in_=qTp[1][:, 0:1024]))
                    g.wait_ge(ldg.sem, ldg.n)
            else:
                @block.sync
                def _(s):
                    s.wait_ge(ss.sem, ev_qk[-1])
                    s.wait_ge(sv.sem, ev_v[-1])
            return nc
        scope1.close()

        oall = [A_(nc.sbuf_tensor(f"oall{p}", [128, N], BF16)) for p in range(2)]
        pbuf = [A_(nc.sbuf_tensor(f"pbuf{i}", [128, 1024], BF16)) for i in range(2)]
        scope2 = top.enter_context(ExitStack())
        S2_ = scope2.enter_context
        psS = [S2_(nc.psum_tensor(f"psS{i}", [128, 1024], F32)) for i in range(2)]
        psOA = S2_(nc.psum_tensor("psOA", [128, 512], F32))
        psOB = S2_(nc.psum_tensor("psOB", [128, 512], F32))
        psEpA = S2_(nc.psum_tensor("psEpA", [128, 512], F32))
        psEpB = S2_(nc.psum_tensor("psEpB", [128, 512], F32))

        PH1_SS = ss.n
        PH1_SV = sv.n
        NIT = 2 * NQT * NKB

        pv_marks = {}
        ob_marks = []
        omul_marks = {}
        ep_prev_ocopy = {}
        rc_marks = {}
        bc_marks = {}
        ecp_marks = {}
        ep_marks = {}

        def emit_qk(t, j, skip_wait=False):
            p, r = j // (NQT * NKB), j % (NQT * NKB)
            qt, kb = r // NKB, r % NKB
            if j >= 2 and not skip_wait:
                t.wait_ge(sxp.sem, j - 1)
            if j == 0:
                t.wait_ge(ss.sem, PH1_SS)
                t.wait_ge(sv.sem, PH1_SV)
            kc = slice(kb * 128, kb * 128 + 128)
            qc = slice(qt * 512, qt * 512 + 512)
            t.matmul(psS[j % 2][:, 0:512], kTp[p][0:64, kc], qTp[p][0:64, qc],
                     start=True, stop=True, tile_position=(0, 0))
            ti = t.matmul(psS[j % 2][:, 512:1024], kTp[p][64:128, kc], qTp[p][64:128, qc],
                          start=True, stop=True, tile_position=(64, 0))
            sqk.inc(ti)

        def emit_pv(t, j):
            p, r = j // (NQT * NKB), j % (NQT * NKB)
            qt, kb = r // NKB, r % NKB
            t.wait_ge(sxp.sem, j + 1)
            if kb == 0 and (p, qt) != (0, 0):
                prev = (p, qt - 1) if qt > 0 else (p - 1, NQT - 1)
                t.wait_ge(sv.sem, ep_marks[prev])
            pb = pbuf[j % 2]
            first = (kb == 0)
            last = (kb == NKB - 1)
            t.matmul(psOA[0:65, :], vh[:, (kb * 4 + 2 * p) * 65:(kb * 4 + 2 * p) * 65 + 65],
                     pb[:, 0:512], start=first, stop=last)
            ti = t.matmul(psOB[0:65, :], vh[:, (kb * 4 + 2 * p + 1) * 65:(kb * 4 + 2 * p + 1) * 65 + 65],
                          pb[:, 512:1024], start=first, stop=last)
            if last:
                st.inc(ti)
                pv_marks[(p, qt)] = st.n

        def emit_ep_recip(pe, qe):
            @block.vector
            def _(v):
                v.wait_ge(st.sem, pv_marks[(pe, qe)])
                if (pe, qe) in ep_prev_ocopy:
                    v.wait_ge(sv.sem, ep_prev_ocopy[(pe, qe)])
                v.tensor_copy(ocopyA[0:65, :], psOA[0:65, :])
                vi = v.tensor_copy(ocopyB[0:65, :], psOB[0:65, :])
                sv.inc(vi)
                ep_marks[(pe, qe)] = sv.n
                v.wait_ge(sv.sem, sv.n)
                v.reciprocal(rowt[64:65, 0:512], ocopyA[64:65, :])
                vi = v.reciprocal(rowt[64:65, 512:1024], ocopyB[64:65, :])
                sv.inc(vi)
                rc_marks[(pe, qe)] = sv.n

        def emit_ep_bcast(t, pe, qe):
            t.wait_ge(sv.sem, rc_marks[(pe, qe)])
            if (pe, qe) != (0, 0):
                prev = (pe, qe - 1) if qe > 0 else (pe - 1, NQT - 1)
                t.wait_ge(sv.sem, ecp_marks[prev])
            t.matmul(psEpA[0:64, :], ones128[64:65, 0:64], rowt[64:65, 0:512],
                     start=True, stop=True, tile_position=(64, 0))
            ti = t.matmul(psEpB[0:64, :], ones128[64:65, 0:64], rowt[64:65, 512:1024],
                          start=True, stop=True, tile_position=(64, 0))
            st.inc(ti)
            bc_marks[(pe, qe)] = st.n

        def emit_ep_mult(pe, qe):
            @block.vector
            def _(v):
                v.wait_ge(st.sem, bc_marks[(pe, qe)])
                if ob_marks:
                    v.wait_ge(lds.sem, ob_marks[-1])
                v.tensor_copy(erow[0:64, :], psEpA[0:64, :])
                vi = v.tensor_copy(erow2[0:64, :], psEpB[0:64, :])
                sv.inc(vi)
                ecp_marks[(pe, qe)] = sv.n
                v.wait_ge(sv.sem, ecp_marks[(pe, qe)])
                qc = slice(qe * 512, qe * 512 + 512)
                v.tensor_tensor(oall[pe][0:64, qc], ocopyA[0:64, :], erow[0:64, :], ALU.mult)
                vi = v.tensor_tensor(obuf2[0:64, :], ocopyB[0:64, :], erow2[0:64, :], ALU.mult)
                sv.inc(vi)
                omul_marks[(pe, qe)] = sv.n
                nxt = (pe, qe + 1) if qe + 1 < NQT else (pe + 1, 0)
                ep_prev_ocopy[nxt] = sv.n

            @block.sync
            def _(s):
                s.wait_ge(sv.sem, omul_marks[(pe, qe)])
                qc = slice(qe * 512, qe * 512 + 512)
                lds.dma(s.dma_start(out=oall[pe][64:128, qc], in_=obuf2[0:64, :]))
                ob_marks.append(lds.n)

        for p in range(2):
            for qt in range(NQT):
                j0 = (p * NQT + qt) * NKB
                prev = None if (p, qt) == (0, 0) else ((p, qt - 1) if qt > 0 else (p - 1, NQT - 1))

                @block.tensor
                def _(t):
                    emit_qk(t, j0)
                    if j0 >= 1:
                        emit_pv(t, j0 - 1)
                if prev is not None:
                    emit_ep_recip(*prev)

                    @block.tensor
                    def _(t):
                        emit_ep_bcast(t, *prev)
                    emit_ep_mult(*prev)

                @block.tensor
                def _(t):
                    for kb in range(1, NKB):
                        emit_qk(t, j0 + kb)
                        emit_pv(t, j0 + kb - 1)

                @block.scalar
                def _(sc):
                    for kb in range(NKB):
                        j = j0 + kb
                        sc.wait_ge(sqk.sem, j + 1)
                        si = sc.activation(pbuf[j % 2][:, :], psS[j % 2][:, 0:1024], AF.Exp,
                                           bias=c_expb[:, 0:1], scale=c_exps[:, 0:1])
                        sxp.inc(si)

        @block.tensor
        def _(t):
            emit_pv(t, NIT - 1)
        emit_ep_recip(1, NQT - 1)

        @block.tensor
        def _(t):
            emit_ep_bcast(t, 1, NQT - 1)
        emit_ep_mult(1, NQT - 1)
        EP_FINAL = omul_marks[(1, NQT - 1)]

        @block.sync
        def _(s):
            s.wait_ge(sv.sem, EP_FINAL)
            lds.dma(s.dma_start(out=cc_att_in[0:128, :], in_=oall[0][:, :]))
            lds.dma(s.dma_start(out=cc_att_in[128:256, :], in_=oall[1][:, :]))
        LDS_OALL = lds.n

        @block.gpsimd
        def _(g):
            g.wait_ge(lds.sem, LDS_OALL)
            gi = g.collective_compute(
                "AllGather", ALU.bypass, replica_groups=[[0, 1, 2, 3], [4, 5, 6, 7]],
                ins=[cc_att_in[:, :]], outs=[cc_att_out[:, :]])
            cc.inc(gi)
        CC_ATT = cc.n

        if upto <= 2:
            if dbg:
                @block.gpsimd
                def _(g):
                    g.wait_ge(cc.sem, CC_ATT)
                    ldg.dma(g.dma_start(out=dbg1[:, 0:2048], in_=oall[0][:, 0:2048]))
                    ldg.dma(g.dma_start(out=dbg2[:, 0:2048], in_=oall[1][:, 0:2048]))
                    g.wait_ge(ldg.sem, ldg.n)
            else:
                @block.sync
                def _(s):
                    s.wait_ge(cc.sem, CC_ATT)
            return nc

        PH2_ST = st.n
        PH2_SS = ss.n
        scope2.close()
        scopeA.close()

        scope3 = top.enter_context(ExitStack())
        S3_ = scope3.enter_context
        wprojb = S3_(nc.sbuf_tensor("wprojb", [128, 8 * 1024], BF16))
        ccTt = S3_(nc.sbuf_tensor("ccTt", [128, 8 * 512], BF16))
        aprojT = S3_(nc.sbuf_tensor("aprojT", [128, 8 * 512], F32))
        xres_t = [S3_(nc.sbuf_tensor(f"xres{i}", [128, 1024], F32)) for i in range(2)]
        x1b = S3_(nc.sbuf_tensor("x1b", [128, NTT * 1024], BF16))
        x1Tt = S3_(nc.sbuf_tensor("x1Tt", [128, 1024], F32))
        xf_t = [S3_(nc.sbuf_tensor(f"xf_t{i}", [128, 1024], F32)) for i in range(2)]
        idxg = S3_(nc.sbuf_tensor("idxg", [128, 32], I32))
        toffb = S3_(nc.sbuf_tensor("toffb", [128, 32], I32))
        scope3m = top.enter_context(ExitStack())
        S3m_ = scope3m.enter_context
        psPr = S3m_(nc.psum_tensor("psPr", [128, 2048], F32))
        psTr = S3m_(nc.psum_tensor("psTr", [128, 2048], F32))

        @block.gpsimd
        def _(g):
            g.wait_ge(lds.sem, LDS_OALL)
            ldg.dma(g.dma_start(out=toffb[:, :], in_=toff_d))
            for ccc in range(8):
                for tt4 in range(4):
                    gi = g.iota(idxg[:, ccc * 4 + tt4:ccc * 4 + tt4 + 1], pattern=[[0, 1]],
                                base=ccc * 128 * N + tt4 * 512, channel_multiplier=N)
            sg.inc(gi)
            ldg.dma(g.dma_start(out=wprojb[:, :].rearrange("p (cc o) -> p cc o", cc=8),
                                in_=wproj_d.rearrange("(cc p) o -> p cc o", p=128)))
        SG_IDX = sg.n
        LDG_WPROJ = ldg.n

        @block.vector
        def _(v):
            v.wait_ge(sg.sem, SG_IDX)
            v.wait_ge(ldg.sem, LDG_WPROJ)
            vi = v.tensor_tensor(idxg[:, :], idxg[:, :], toffb[:, :], ALU.add)
            sv.inc(vi)
        SV_IDX = sv.n

        @block.tensor
        def _(t):
            t.wait_ge(lds.sem, LDS_W0)
            t.wait_ge(ss.sem, PH2_SS)
            t.wait_ge(sv.sem, EP_FINAL)
            t.wait_ge(sg.sem, SG_CONST)
            ti = t.transpose(psTr[:, 0:8], bprojS[0:8, :], identf[0:8, 0:8])
            st.inc(ti)
        ST_BPJ = st.n

        @block.vector
        def _(v):
            v.wait_ge(st.sem, ST_BPJ)
            vi = v.tensor_copy(bprojT[:, 0:8], psTr[:, 0:8])
            sv.inc(vi)
        SV_BPJ = sv.n

        x1_done = []
        ldg_cct = []
        pr_evac = []
        prmm_last = []
        for tt4 in range(4):
            @block.gpsimd
            def _(g):
                g.wait_ge(cc.sem, CC_ATT)
                g.wait_ge(sv.sem, SV_IDX)
                if tt4 >= 1:
                    g.wait_ge(st.sem, prmm_last[tt4 - 1])
                for ccc in range(8):
                    ldg.dma(g.indirect_dma_start(
                        out=ccTt[:, ccc * 512:(ccc + 1) * 512], out_offset=None,
                        in_=cc_att_out[:, :],
                        in_offset=IOA(idxg[:, ccc * 4 + tt4:ccc * 4 + tt4 + 1], 1)))
            ldg_cct.append(ldg.n)

            ss_base = ss.n
            prmm_marks = []
            for cs in range(8):
                @block.tensor
                def _(t):
                    t.wait_ge(ldg.sem, ldg_cct[tt4])
                    if tt4 == 0 and cs == 0:
                        t.wait_ge(sv.sem, SV_BPJ)
                    gi = tt4 * 8 + cs
                    if gi >= 4:
                        t.wait_ge(ss.sem, ss_base + cs - 3 if cs >= 4 else pr_evac[gi - 4])
                    bank = psPr[:, (cs % 4) * 512:(cs % 4) * 512 + 512]
                    for ccc in range(8):
                        ti = t.matmul(bank,
                                      wprojb[:, ccc * 1024 + cs * 128:ccc * 1024 + cs * 128 + 128],
                                      ccTt[:, ccc * 512:(ccc + 1) * 512],
                                      start=(ccc == 0), stop=(ccc == 7))
                    st.inc(ti)
                prmm_marks.append(st.n)
                if cs == 7:
                    prmm_last.append(st.n)

            @block.scalar
            def _(sc):
                for cs in range(8):
                    sc.wait_ge(st.sem, prmm_marks[cs])
                    bank = psPr[:, (cs % 4) * 512:(cs % 4) * 512 + 512]
                    si = sc.activation(aprojT[:, cs * 512:(cs + 1) * 512], bank, AF.Identity,
                                       bias=bprojT[:, cs:cs + 1], scale=c_one[:, 0:1])
                    ss.inc(si)
                    pr_evac.append(ss.n)

            for tsub in range(4):
                ti128 = tt4 * 4 + tsub

                @block.sync
                def _(s):
                    if ti128 >= 2:
                        s.wait_ge(sv.sem, x1_done[ti128 - 2])
                    lds.dma(s.dma_start(out=xres_t[tsub % 2][:, :],
                                        in_=xres_d[ti128 * 128:(ti128 + 1) * 128, :]))
                LDS_XR = lds.n

                @block.tensor
                def _(t):
                    t.wait_ge(ss.sem, pr_evac[tt4 * 8 + 7])
                    if ti128 >= 2:
                        t.wait_ge(sv.sem, x1_done[ti128 - 2])
                    half = psTr[:, (tsub % 2) * 1024:(tsub % 2) * 1024 + 1024]
                    for cs in range(8):
                        ti = t.transpose(half[:, cs * 128:(cs + 1) * 128],
                                         aprojT[:, cs * 512 + tsub * 128:cs * 512 + tsub * 128 + 128],
                                         identf[:, :])
                    st.inc(ti)
                ST_TR = st.n

                @block.vector
                def _(v):
                    v.wait_ge(st.sem, ST_TR)
                    v.wait_ge(lds.sem, LDS_XR)
                    half = psTr[:, (tsub % 2) * 1024:(tsub % 2) * 1024 + 1024]
                    vi = v.tensor_tensor(scr0[:, :], half[:, 0:1024], xres_t[tsub % 2][:, :], ALU.add)
                    sv.inc(vi)
                    v.wait_ge(sv.sem, sv.n)
                    vi = v.reduce_sum(col0[:, :], scr0[:, :], axis=AX.X)
                    sv.inc(vi)
                    v.wait_ge(sv.sem, sv.n)
                    vi = v.tensor_scalar_mul(col1[:, :], col0[:, :], INV_D)
                    sv.inc(vi)
                    v.wait_ge(sv.sem, sv.n)
                    vi = v.tensor_scalar(scr1[:, :], scr0[:, :], col1[:, 0:1], None, ALU.subtract)
                    sv.inc(vi)
                SV_XC = sv.n

                @block.scalar
                def _(sc):
                    sc.wait_ge(sv.sem, SV_XC)
                    si = sc.activation(scr0[:, :], scr1[:, :], AF.Square,
                                       bias=c_zero[:, 0:1], scale=c_one[:, 0:1],
                                       accum_out=col2[:, :])
                    ss.inc(si)
                    sc.wait_ge(ss.sem, ss.n)
                    si = sc.activation(col3[:, :], col2[:, :], AF.Ln,
                                       bias=c_eps[:, 0:1], scale=c_invd[:, 0:1])
                    ss.inc(si)
                    sc.wait_ge(ss.sem, ss.n)
                    si = sc.activation(col3[:, :], col3[:, :], AF.Exp,
                                       bias=c_zero[:, 0:1], scale=c_half[:, 0:1])
                    ss.inc(si)
                SS_SQ = ss.n - 2
                SS_RSTD = SS_SQ + 2

                @block.vector
                def _(v):
                    v.wait_ge(ss.sem, SS_RSTD)
                    vi = v.tensor_scalar(scr0[:, :], scr1[:, :], col3[:, 0:1], None, ALU.mult)
                    sv.inc(vi)
                    v.wait_ge(sv.sem, sv.n)
                    vi = v.tensor_copy(x1b[:, ti128 * 1024:(ti128 + 1) * 1024], scr0[:, :])
                    sv.inc(vi)
                SV_X1 = sv.n

                @block.sync
                def _(s):
                    s.wait_ge(sv.sem, SV_X1)
                    lds.dma(s.dma_start(out=x1spill[ti128 * 128:(ti128 + 1) * 128, :],
                                        in_=scr0[:, :]))
                x1_done.append(SV_X1)

        if upto <= 3:
            @block.sync
            def _(s):
                s.wait_ge(lds.sem, lds.n)
                s.wait_ge(sv.sem, sv.n)
            return nc
        scope3m.close()
        scope4a = top.enter_context(ExitStack())
        S4a_ = scope4a.enter_context
        psLgT = S4a_(nc.psum_tensor("psLgT", [128, 1024], F32))
        psLg = S4a_(nc.psum_tensor("psLg", [128, 512], F32))
        psCnt = S4a_(nc.psum_tensor("psCnt", [1, 512], F32))
        psPos = S4a_(nc.psum_tensor("psPos", [128, 512], F32))

        g_marks = []
        lg_marks = []
        x1t_marks = []
        x1t_marks_v = []
        PH3_ST = st.n
        PH3_SS = ss.n
        for k in range(NTT):
            @block.sync
            def _(s):
                if k >= 2:
                    s.wait_ge(st.sem, x1t_marks[k - 2])
                lds.dma(s.dma_start(out=xf_t[k % 2][:, :],
                                    in_=x1spill[k * 128:(k + 1) * 128, :]))
            LDS_XF = lds.n

            @block.tensor
            def _(t):
                t.wait_ge(sv.sem, x1_done[k])
                t.wait_ge(lds.sem, LDS_XF)
                if k == 0:
                    t.wait_ge(ss.sem, PH3_SS)
                else:
                    t.wait_ge(sv.sem, x1t_marks_v[k - 1])
                for ccc in range(8):
                    ti = t.transpose(psLgT[:, ccc * 128:(ccc + 1) * 128],
                                     xf_t[k % 2][:, ccc * 128:(ccc + 1) * 128],
                                     identf[:, :])
                st.inc(ti)
                x1t_marks.append(st.n)
            ST_X1T = st.n

            @block.vector
            def _(v):
                v.wait_ge(st.sem, ST_X1T)
                vi = v.tensor_copy(x1Tt[:, :], psLgT[:, 0:1024])
                sv.inc(vi)
                x1t_marks_v.append(sv.n)
            SV_X1T = sv.n

            @block.tensor
            def _(t):
                t.wait_ge(sv.sem, SV_X1T)
                if k >= 1:
                    t.wait_ge(sv.sem, lg_marks[k - 1])
                for ccc in range(8):
                    t.matmul(psLg[:, 0:NE], x1Tt[:, ccc * 128:(ccc + 1) * 128],
                             wgb[:, ccc * NE:(ccc + 1) * NE],
                             start=(ccc == 0), stop=False)
                ti = t.matmul(psLg[:, 0:NE], ones128[0:1, :], bgb[:, :], start=False, stop=True)
                st.inc(ti)
            ST_LG = st.n

            @block.vector
            def _(v):
                v.wait_ge(st.sem, ST_LG)
                vi = v.tensor_copy(Lg[:, :], psLg[:, 0:NE])
                sv.inc(vi)
                lg_marks.append(sv.n)
                v.wait_ge(sv.sem, sv.n)
                vi = v.reduce_max(col0[:, :], Lg[:, :], axis=AX.X)
                sv.inc(vi)
                v.wait_ge(sv.sem, sv.n)
                vi = v.tensor_scalar_mul(col1[:, :], col0[:, :], -1.0)
                sv.inc(vi)
                vi = v.tensor_scalar(ohw[:, :], Lg[:, :], col0[:, 0:1], None, ALU.is_ge)
                sv.inc(vi)
            SV_OH = sv.n

            @block.scalar
            def _(sc):
                sc.wait_ge(sv.sem, SV_OH)
                si = sc.activation(expl[:, :], Lg[:, :], AF.Exp,
                                   bias=col1[:, 0:1], scale=c_one[:, 0:1],
                                   accum_out=col2[:, :])
                ss.inc(si)
            SS_EXPL = ss.n

            @block.vector
            def _(v):
                v.wait_ge(ss.sem, SS_EXPL)
                vi = v.reciprocal(gates[:, k:k + 1], col2[:, :])
                sv.inc(vi)
                vi = v.tensor_copy(ohs[:, k * NE:(k + 1) * NE], ohw[:, :])
                sv.inc(vi)
                v.wait_ge(sv.sem, sv.n)
                vi = v.tensor_tensor(expl[:, :], ohw[:, :], iota16[:, :], ALU.mult)
                sv.inc(vi)
                v.wait_ge(sv.sem, sv.n)
                vi = v.reduce_sum(idxs[:, k:k + 1], expl[:, :], axis=AX.X)
                sv.inc(vi)
                if k >= 1:
                    v.wait_ge(st.sem, g_marks[k - 1])
                    vi = v.tensor_copy(basetile[:, k * NE:(k + 1) * NE], psCnt[0:1, 0:NE])
                    sv.inc(vi)
            SV_BT = sv.n

            @block.tensor
            def _(t):
                t.wait_ge(sv.sem, SV_BT)
                ti = t.matmul(psCnt[0:1, 0:NE], c_one[:, 0:1], ohs[:, k * NE:(k + 1) * NE],
                              start=(k == 0), stop=(k == NTT - 1))
                st.inc(ti)
            g_marks.append(st.n)

        @block.vector
        def _(v):
            v.wait_ge(st.sem, g_marks[-1])
            vi = v.tensor_copy(cntrow[:, :], psCnt[0:1, 0:NE])
            sv.inc(vi)
        SV_CNT = sv.n

        @block.sync
        def _(s):
            s.wait_ge(sv.sem, SV_CNT)
            lds.dma(s.dma_start(out=cc_cnt_in.ap(), in_=cntrow[:, :]))
        LDS_CNT = lds.n

        @block.gpsimd
        def _(g):
            g.wait_ge(lds.sem, LDS_CNT)
            gi = g.collective_compute("AllGather", ALU.bypass, replica_groups=[list(range(8))],
                                      ins=[cc_cnt_in[:, :]], outs=[cc_cnt_out[:, :]])
            cc.inc(gi)
            g.wait_ge(cc.sem, cc.n)
            ldg.dma(g.dma_start(out=cnts_all[:, :], in_=cc_cnt_out.ap()))
        LDG_CNTS = ldg.n

        @block.tensor
        def _(t):
            t.wait_ge(ldg.sem, LDG_CNTS)
            t.wait_ge(sv.sem, SV_CNT)
            ti = t.matmul(psCnt[0:1, 16:32], rankm[:, :], cnts_all[:, :],
                          start=True, stop=True)
            st.inc(ti)
        ST_BC = st.n

        @block.vector
        def _(v):
            v.wait_ge(st.sem, ST_BC)
            vi = v.tensor_copy(basecore[:, :], psCnt[0:1, 16:32])
            sv.inc(vi)
        SV_BC = sv.n

        sc_marks = []
        for k in range(NTT):
            @block.vector
            def _(v):
                v.wait_ge(sv.sem, SV_BC)
                if k >= 1:
                    v.wait_ge(st.sem, sc_marks[k - 1][0])
                vi = v.tensor_tensor(mrow[:, :], basecore[:, :],
                                     basetile[:, k * NE:(k + 1) * NE], ALU.add)
                sv.inc(vi)
            SV_MR = sv.n

            @block.tensor
            def _(t):
                t.wait_ge(sv.sem, SV_MR)
                t.matmul(psPos[:, 0:NE], triu[:, :], ohs[:, k * NE:(k + 1) * NE],
                         start=True, stop=False)
                ti = t.matmul(psPos[:, 0:NE], ones128[0:1, :], mrow[:, :],
                              start=False, stop=True)
                st.inc(ti)
            ST_POS = st.n

            @block.vector
            def _(v):
                v.wait_ge(st.sem, ST_POS)
                vi = v.tensor_tensor(expl[:, :], psPos[:, 0:NE], ohs[:, k * NE:(k + 1) * NE],
                                     ALU.mult)
                sv.inc(vi)
                v.wait_ge(sv.sem, sv.n)
                vi = v.reduce_sum(posf[:, :], expl[:, :], axis=AX.X)
                sv.inc(vi)
                v.wait_ge(sv.sem, sv.n)
                vi = v.tensor_scalar(keepm[:, :], posf[:, :], float(CAP), None, ALU.is_lt)
                sv.inc(vi)
                vi = v.tensor_scalar_mul(slotf[:, :], idxs[:, k:k + 1], float(CAP))
                sv.inc(vi)
                v.wait_ge(sv.sem, sv.n)
                vi = v.tensor_tensor(slotf[:, :], slotf[:, :], posf[:, :], ALU.add)
                sv.inc(vi)
                v.wait_ge(sv.sem, sv.n)
                vi = v.tensor_scalar(col0[:, :], keepm[:, :], -BIG, BIG, ALU.mult, ALU.add)
                sv.inc(vi)
                vi = v.tensor_tensor(slotg[:, :], slotf[:, :], keepm[:, :], ALU.mult)
                sv.inc(vi)
                v.wait_ge(sv.sem, sv.n)
                vi = v.tensor_tensor(slotg[:, :], slotg[:, :], col0[:, :], ALU.add)
                sv.inc(vi)
                v.wait_ge(sv.sem, sv.n)
                vi = v.tensor_copy(slots[:, k:k + 1], slotg[:, :])
                sv.inc(vi)
            SV_SLOT = sv.n

            @block.gpsimd
            def _(g):
                g.wait_ge(sv.sem, SV_SLOT)
                if k == 0:
                    g.wait_ge(lds.sem, LDS_ZERO)
                ldg.dma(g.indirect_dma_start(
                    out=xsort_loc[:, :], out_offset=IOA(slots[:, k:k + 1], 0),
                    in_=x1b[:, k * 1024:(k + 1) * 1024], in_offset=None,
                    bounds_check=NTOK - 1, oob_is_err=False))
            sc_marks.append((st.n, ldg.n))
        LDG_SCAT = ldg.n

        @block.gpsimd
        def _(g):
            g.wait_ge(ldg.sem, LDG_SCAT)
            gi = g.collective_compute("ReduceScatter", ALU.add, replica_groups=[list(range(8))],
                                      ins=[xsort_loc[:, :]], outs=[xsort[:, :]])
            cc.inc(gi)
        CC_RS = cc.n
        if upto <= 4:
            if dbg:
                @block.sync
                def _(s):
                    s.wait_ge(cc.sem, CC_RS)
                    lds.dma(s.dma_start(out=dbg1[:, 0:16], in_=gates[:, :]))
                    lds.dma(s.dma_start(out=dbg2[:, 0:16], in_=idxs[:, :]))
                    lds.dma(s.dma_start(out=dbgxs, in_=xsort.ap()))
                    lds.dma(s.dma_start(out=dbgsl, in_=slots[:, :]))
                    s.wait_ge(lds.sem, lds.n)
            else:
                @block.sync
                def _(s):
                    s.wait_ge(cc.sem, CC_RS)
            return nc

        PH4A_ST = st.n
        PH4A_SS = ss.n
        PH4A_SV = sv.n
        scope4a.close()
        scope3.close()

        scope4b = top.enter_context(ExitStack())
        S4b_ = scope4b.enter_context
        bufT = S4b_(nc.sbuf_tensor("bufT", [128, 8 * 1024], BF16))
        hT = S4b_(nc.sbuf_tensor("hT", [128, 32 * 1024], BF16))
        w2f = S4b_(nc.sbuf_tensor("w2f", [128, 32 * 1024], BF16))
        w1b = [S4b_(nc.sbuf_tensor(f"w1b{i}", [128, 8 * 512], BF16)) for i in range(2)]
        bufS = [S4b_(nc.sbuf_tensor(f"bufS{i}", [128, 1024], BF16)) for i in range(2)]
        b1S = S4b_(nc.sbuf_tensor("b1S", [32, 128], F32))
        b1T = S4b_(nc.sbuf_tensor("b1T", [128, 32], F32))
        b2S = S4b_(nc.sbuf_tensor("b2S", [8, 128], F32))
        b2T = S4b_(nc.sbuf_tensor("b2T", [128, 8], F32))
        ytmp = S4b_(nc.sbuf_tensor("ytmp", [128, 512], F32))
        ytok = S4b_(nc.sbuf_tensor("ytok", [128, 4 * 1024], BF16))

        lds_y = []
        prev_h_last = 0
        prev_y_last = 0
        for e in range(EPC):
            scope_p = top.enter_context(ExitStack())
            psBufT = scope_p.enter_context(nc.psum_tensor(f"psBufT{e}", [128, 1024], BF16))
            psTb = scope_p.enter_context(nc.psum_tensor(f"psTb{e}", [128, 512], F32))

            @block.sync
            def _(s):
                s.wait_ge(ldg.sem, LDG_SCAT)
                lds.dma(s.dma_start(out=b1S[:, :], in_=b1_d[e]))
                lds.dma(s.dma_start(out=b2S[:, :], in_=b2_d[e]))
            LDS_B = lds.n

            @block.tensor
            def _(t):
                t.wait_ge(lds.sem, LDS_B)
                if e == 0:
                    t.wait_ge(ss.sem, PH4A_SS)
                    t.wait_ge(sv.sem, PH4A_SV)
                else:
                    t.wait_ge(ss.sem, ss.n)
                    t.wait_ge(sv.sem, sv.n)
                t.transpose(psTb[:, 0:32], b1S[:, :], identf[0:32, 0:32])
                ti = t.transpose(psTb[:, 32:40], b2S[:, :], identf[0:8, 0:8])
                st.inc(ti)
            ST_B = st.n

            @block.vector
            def _(v):
                v.wait_ge(st.sem, ST_B)
                v.tensor_copy(b1T[:, :], psTb[:, 0:32])
                vi = v.tensor_copy(b2T[:, :], psTb[:, 32:40])
                sv.inc(vi)
            SV_B = sv.n

            buf_marks = []
            for i in range(8):
                @block.sync
                def _(s):
                    s.wait_ge(cc.sem, CC_RS)
                    if i >= 2:
                        s.wait_ge(sv.sem, buf_marks[i - 2])
                    lds.dma(s.dma_start(
                        out=bufS[i % 2][:, :],
                        in_=xsort[(e * CAP + i * 128):(e * CAP + (i + 1) * 128), :]))
                LDS_BUF = lds.n

                @block.tensor
                def _(t):
                    t.wait_ge(lds.sem, LDS_BUF)
                    if i == 0:
                        t.wait_ge(sv.sem, SV_B)
                    if i >= 1:
                        t.wait_ge(sv.sem, buf_marks[i - 1])
                    for ccc in range(8):
                        ti = t.transpose(psBufT[:, ccc * 128:(ccc + 1) * 128],
                                         bufS[i % 2][:, ccc * 128:(ccc + 1) * 128],
                                         identb[:, :])
                    st.inc(ti)
                ST_BUF = st.n

                @block.vector
                def _(v):
                    v.wait_ge(st.sem, ST_BUF)
                    for ccc in range(8):
                        vi = v.tensor_copy(bufT[:, ccc * 1024 + i * 128:ccc * 1024 + (i + 1) * 128],
                                           psBufT[:, ccc * 128:(ccc + 1) * 128])
                    sv.inc(vi)
                    buf_marks.append(sv.n)
            SV_BUFT = sv.n

            if dbg and upto == 5 and e == 0 and not _NODUMP:
                @block.gpsimd
                def _(g):
                    g.wait_ge(sv.sem, SV_BUFT)
                    ldg.dma(g.dma_start(out=dbgbt[:, :], in_=bufT[:, 0:2048]))
            scope_p.close()
            scope_h = top.enter_context(ExitStack())
            psH = scope_h.enter_context(nc.psum_tensor(f"psH{e}", [128, 2048], F32))

            h_mm = []
            ss_base_h = ss.n
            for hg in range(8):
                @block.gpsimd
                def _(g):
                    if hg == 0:
                        if e == 0:
                            g.wait_ge(ldg.sem, LDG_SCAT)
                        else:
                            g.wait_ge(st.sem, prev_h_last)
                    if hg >= 2:
                        g.wait_ge(st.sem, h_mm[-1])
                    ldg.dma(g.dma_start(
                        out=w1b[hg % 2][:, :].rearrange("p (cc h) -> p cc h", cc=8),
                        in_=w1_d[e, :, hg * 512:(hg + 1) * 512].rearrange("(cc p) h -> p cc h", p=128)))
                LDG_W1 = ldg.n

                @block.gpsimd
                def _(g):
                    if e > 0 and hg == 0:
                        g.wait_ge(st.sem, prev_y_last)
                    ldg.dma(g.dma_start(
                        out=w2f[:, hg * 4096:(hg + 1) * 4096].rearrange("p (q c) -> p q c", q=4),
                        in_=w2_d[e, hg * 512:(hg + 1) * 512, :].rearrange("(q p) c -> p q c", p=128)))
                LDG_W2F = ldg.n

                for hs4 in range(4):
                    for sti in range(2):
                        bsel = (hs4 * 2 + sti) % 4

                        @block.tensor
                        def _(t):
                            t.wait_ge(ldg.sem, LDG_W1)
                            if hg == 0 and hs4 == 0 and sti == 0:
                                t.wait_ge(sv.sem, SV_BUFT)
                            gi = hg * 8 + hs4 * 2 + sti
                            if gi >= 4:
                                t.wait_ge(ss.sem, ss_base_h + gi - 3)
                            bank = psH[:, bsel * 512:(bsel + 1) * 512]
                            for ccc in range(8):
                                ti = t.matmul(bank,
                                              w1b[hg % 2][:, ccc * 512 + hs4 * 128:ccc * 512 + hs4 * 128 + 128],
                                              bufT[:, ccc * 1024 + sti * 512:ccc * 1024 + sti * 512 + 512],
                                              start=(ccc == 0), stop=(ccc == 7))
                            st.inc(ti)
                            h_mm.append(st.n)

            @block.scalar
            def _(sc):
                for gi in range(64):
                    hg, hs4, sti = gi // 8, (gi % 8) // 2, gi % 2
                    bsel = (hs4 * 2 + sti) % 4
                    sc.wait_ge(st.sem, h_mm[gi])
                    bank = psH[:, bsel * 512:(bsel + 1) * 512]
                    hs = hg * 4 + hs4
                    si = sc.activation(hT[:, hs * 1024 + sti * 512:hs * 1024 + sti * 512 + 512],
                                       bank, AF.Relu,
                                       bias=b1T[:, hs:hs + 1], scale=c_one[:, 0:1])
                    ss.inc(si)
            SS_H = ss.n
            prev_h_last = h_mm[-1]

            if dbg and upto == 5 and e == 0 and not _NODUMP:
                @block.gpsimd
                def _(g):
                    g.wait_ge(ss.sem, SS_H)
                    ldg.dma(g.dma_start(out=dbgh[:, 0:2048], in_=hT[:, 0:2048]))
                    ldg.dma(g.dma_start(out=dbgh[:, 2048:4096], in_=hT[:, 16 * 1024:16 * 1024 + 2048]))
            scope_h.close()
            scope_y = top.enter_context(ExitStack())
            psY = scope_y.enter_context(nc.psum_tensor(f"psY{e}", [128, 4096], F32))

            for sti in range(2):
                y_mm = []

                @block.tensor
                def _(t):
                    t.wait_ge(ldg.sem, LDG_W2F)
                    if sti == 0:
                        t.wait_ge(ss.sem, SS_H)
                    if sti == 1:
                        t.wait_ge(sv.sem, sv.n)
                    for cs in range(8):
                        bank = psY[:, cs * 512:(cs + 1) * 512]
                        for hc in range(32):
                            ti = t.matmul(bank,
                                          w2f[:, hc * 1024 + cs * 128:hc * 1024 + cs * 128 + 128],
                                          hT[:, hc * 1024 + sti * 512:hc * 1024 + sti * 512 + 512],
                                          start=(hc == 0), stop=(hc == 31))
                    st.inc(ti)
                    y_mm.append(st.n)
                Y_MM = y_mm[-1]
                ss_base_y = ss.n
                st_base_y = st.n

                @block.scalar
                def _(sc):
                    for cs in range(8):
                        sc.wait_ge(st.sem, Y_MM)
                        if cs >= 1:
                            sc.wait_ge(st.sem, st_base_y + cs)
                        bank = psY[:, cs * 512:(cs + 1) * 512]
                        si = sc.activation(ytmp[:, :], bank, AF.Identity,
                                           bias=b2T[:, cs:cs + 1], scale=c_one[:, 0:1])
                        ss.inc(si)

                for cs in range(8):
                    @block.tensor
                    def _(t):
                        t.wait_ge(ss.sem, ss_base_y + cs + 1)
                        for tb in range(4):
                            ti = t.transpose(psY[:, cs * 512 + tb * 128:cs * 512 + tb * 128 + 128],
                                             ytmp[:, tb * 128:(tb + 1) * 128],
                                             identf[:, :])
                        st.inc(ti)
                    ST_YT = st.n

                    @block.vector
                    def _(v):
                        v.wait_ge(st.sem, ST_YT)
                        if cs == 0 and lds_y:
                            v.wait_ge(lds.sem, lds_y[-1])
                        for tb in range(4):
                            vi = v.tensor_copy(ytok[:, tb * 1024 + cs * 128:tb * 1024 + cs * 128 + 128],
                                               psY[:, cs * 512 + tb * 128:cs * 512 + tb * 128 + 128])
                        sv.inc(vi)
                SV_YTOK = sv.n

                @block.sync
                def _(s):
                    s.wait_ge(sv.sem, SV_YTOK)
                    for tb in range(4):
                        r0 = e * CAP + sti * 512 + tb * 128
                        lds.dma(s.dma_start(out=cc_y_in[r0:r0 + 128, :],
                                            in_=ytok[:, tb * 1024:(tb + 1) * 1024]))
                lds_y.append(lds.n)

            prev_y_last = st.n
            scope_y.close()

        @block.gpsimd
        def _(g):
            g.wait_ge(lds.sem, lds_y[-1])
            gi = g.collective_compute("AllGather", ALU.bypass, replica_groups=[list(range(8))],
                                      ins=[cc_y_in[:, :]], outs=[cc_y_out[:, :]])
            cc.inc(gi)
        CC_Y = cc.n
        if upto <= 5:
            @block.sync
            def _(s):
                s.wait_ge(cc.sem, CC_Y)
                if dbg:
                    lds.dma(s.dma_start(out=dbgy, in_=cc_y_in.ap()))
                    lds.dma(s.dma_start(out=dbgxs5, in_=xsort.ap()))
                    s.wait_ge(lds.sem, lds.n)
            if dbg:
                @block.gpsimd
                def _(g):
                    g.wait_ge(cc.sem, CC_Y)
                    g.wait_ge(ldg.sem, ldg.n)
            return nc

        PH4B_SS = ss.n
        PH4B_SV = sv.n
        scope4b.close()

        scope4c = top.enter_context(ExitStack())
        S4c_ = scope4c.enter_context
        ygath = [S4c_(nc.sbuf_tensor(f"ygath{i}", [128, 1024], BF16)) for i in range(2)]
        x1t = [S4c_(nc.sbuf_tensor(f"x1t{i}", [128, 1024], F32)) for i in range(2)]
        outt = [S4c_(nc.sbuf_tensor(f"outt{i}", [128, 1024], F32)) for i in range(2)]

        yz_marks = []
        so_marks = []
        for k in range(NTT):
            @block.vector
            def _(v):
                if k == 0:
                    v.wait_ge(ss.sem, PH4B_SS)
                    v.wait_ge(lds.sem, lds_y[-1])
                if k >= 2:
                    v.wait_ge(ldg.sem, yz_marks[k - 2][1])
                vi = v.memset(ygath[k % 2][:, :], 0.0)
                sv.inc(vi)
            SV_YZ = sv.n

            @block.gpsimd
            def _(g):
                g.wait_ge(cc.sem, CC_Y)
                g.wait_ge(sv.sem, SV_YZ)
                ldg.dma(g.indirect_dma_start(
                    out=ygath[k % 2][:, :], out_offset=None,
                    in_=cc_y_out[:, :], in_offset=IOA(slots[:, k:k + 1], 0),
                    bounds_check=NTOK - 1, oob_is_err=False))
            LDG_YG = ldg.n
            yz_marks.append((SV_YZ, LDG_YG))

            @block.sync
            def _(s):
                if k >= 2:
                    s.wait_ge(sv.sem, so_marks[k - 2][0])
                lds.dma(s.dma_start(out=x1t[k % 2][:, :],
                                    in_=x1spill[k * 128:(k + 1) * 128, :]))
            LDS_X1T = lds.n

            @block.vector
            def _(v):
                v.wait_ge(ldg.sem, LDG_YG)
                v.wait_ge(lds.sem, LDS_X1T)
                vi = v.tensor_scalar(scr1[:, :], ygath[k % 2][:, :], gates[:, k:k + 1],
                                     None, ALU.mult)
                sv.inc(vi)
                v.wait_ge(sv.sem, sv.n)
                vi = v.tensor_tensor(scr0[:, :], x1t[k % 2][:, :], scr1[:, :], ALU.add)
                sv.inc(vi)
                v.wait_ge(sv.sem, sv.n)
                vi = v.reduce_sum(col0[:, :], scr0[:, :], axis=AX.X)
                sv.inc(vi)
                v.wait_ge(sv.sem, sv.n)
                vi = v.tensor_scalar_mul(col1[:, :], col0[:, :], INV_D)
                sv.inc(vi)
                v.wait_ge(sv.sem, sv.n)
                vi = v.tensor_scalar(scr1[:, :], scr0[:, :], col1[:, 0:1], None, ALU.subtract)
                sv.inc(vi)
            SV_XC2 = sv.n

            @block.scalar
            def _(sc):
                sc.wait_ge(sv.sem, SV_XC2)
                si = sc.activation(scr0[:, :], scr1[:, :], AF.Square,
                                   bias=c_zero[:, 0:1], scale=c_one[:, 0:1],
                                   accum_out=col2[:, :])
                ss.inc(si)
                sc.wait_ge(ss.sem, ss.n)
                si = sc.activation(col3[:, :], col2[:, :], AF.Ln,
                                   bias=c_eps[:, 0:1], scale=c_invd[:, 0:1])
                ss.inc(si)
                sc.wait_ge(ss.sem, ss.n)
                si = sc.activation(col3[:, :], col3[:, :], AF.Exp,
                                   bias=c_zero[:, 0:1], scale=c_half[:, 0:1])
                ss.inc(si)
            SS_SQ2 = ss.n - 2
            SS_RSTD2 = SS_SQ2 + 2

            @block.vector
            def _(v):
                v.wait_ge(ss.sem, SS_RSTD2)
                if k >= 2:
                    v.wait_ge(so.sem, so_marks[k - 2][1])
                vi = v.tensor_scalar(outt[k % 2][:, :], scr1[:, :], col3[:, 0:1], None, ALU.mult)
                sv.inc(vi)
            SV_OUT = sv.n

            @block.sync
            def _(s):
                s.wait_ge(sv.sem, SV_OUT)
                so.dma(s.dma_start(out=out_d[k * 128:(k + 1) * 128, :], in_=outt[k % 2][:, :]))
            so_marks.append((SV_OUT, so.n))

        @block.sync
        def _(s):
            s.wait_ge(so.sem, so_marks[-1][1])

        scope4c.close()

    return nc


def kernel(L0_attn_x, L0_attn_w_qkv, L0_attn_w_proj, L0_attn_b_proj,
           L0_ffn_Wg, L0_ffn_bg, L0_ffn_W1, L0_ffn_b1, L0_ffn_W2, L0_ffn_b2):
    x = np.asarray(L0_attn_x, dtype=np.float32)
    w_qkv = np.asarray(L0_attn_w_qkv, dtype=np.float32)
    w_proj = np.ascontiguousarray(np.asarray(L0_attn_w_proj, dtype=np.float32))
    b_proj = np.asarray(L0_attn_b_proj, dtype=np.float32)
    Wg = np.ascontiguousarray(np.asarray(L0_ffn_Wg, dtype=np.float32))
    bg = np.asarray(L0_ffn_bg, dtype=np.float32)
    W1 = np.asarray(L0_ffn_W1, dtype=np.float32)
    b1 = np.asarray(L0_ffn_b1, dtype=np.float32)
    W2 = np.asarray(L0_ffn_W2, dtype=np.float32)
    b2 = np.asarray(L0_ffn_b2, dtype=np.float32)

    xt = [np.ascontiguousarray(x[b].T) for b in range(B)]
    in_maps = []
    for c in range(8):
        b = c // 4
        g = c % 4
        h0 = 4 * g
        wq = w_qkv[:, 64 * h0: 64 * h0 + 256]
        wk = w_qkv[:, D + 64 * h0: D + 64 * h0 + 256]
        wv = w_qkv[:, 2 * D + 64 * h0: 2 * D + 64 * h0 + 256]
        wsel = np.ascontiguousarray(
            np.concatenate([wq[:, 0:128], wq[:, 128:256], wk[:, 0:128], wk[:, 128:256], wv], axis=1))
        toff = np.full((128, 32), g * TOK, dtype=np.int32)
        in_maps.append({
            "xt": xt[b],
            "xres": np.ascontiguousarray(x[b, g * TOK:(g + 1) * TOK, :]),
            "wqkv_sel": wsel,
            "wproj": w_proj,
            "bprojr": np.ascontiguousarray(b_proj.reshape(8, 128)),
            "wg": Wg,
            "bg1": np.ascontiguousarray(bg.reshape(1, NE)),
            "w1": np.ascontiguousarray(W1[2 * c:2 * c + 2]),
            "b1r": np.ascontiguousarray(b1[2 * c:2 * c + 2].reshape(EPC, 32, 128)),
            "w2": np.ascontiguousarray(W2[2 * c:2 * c + 2]),
            "b2r": np.ascontiguousarray(b2[2 * c:2 * c + 2].reshape(EPC, 8, 128)),
            "rankmask": (np.arange(8) < c).astype(np.float32).reshape(8, 1),
            "toff": toff,
        })

    nc = build_program()
    res = run_bass_kernel_spmd(nc, in_maps, core_ids=list(range(8)))
    parts = [res.results[c]["out"] for c in range(8)]
    full = np.concatenate(parts, axis=0).reshape(B, N, D)
    return full.astype(np.float32)
